# version 14
# baseline (speedup 1.0000x reference)
import sys
sys.path.insert(0, "/opt/trn_rl_repo")
import contextlib
import numpy as np
import ml_dtypes
import concourse.bass as bass
import concourse.mybir as mybir
from concourse.bass_utils import run_bass_kernel_spmd

F32 = mybir.dt.float32
BF16 = mybir.dt.bfloat16
AF = mybir.ActivationFunctionType
ALU = mybir.AluOpType
AX = mybir.AxisListType

T = 4096
C = 1024
NH = 16
HS = 64
NT = T // 128
NG = C // 128
QG = T // 512
KC = C // 128
XTP = 4100
EPS = 1e-5

PARANOID = False
ENGS = ("tensor", "vector", "scalar", "gpsimd", "sync")


class Ctx:
    def __init__(self, nc, es):
        self.nc = nc
        self.prog = {e: [] for e in ENGS}
        self.sems = {}
        self.cnt = {}
        for name in ("qs", "qg", "qt", "pe", "dve", "act", "pool",
                     "px0", "px1", "pf0", "pf1", "pst", "pk0", "pk1",
                     "p2a", "p2b", "p2c", "p2d", "p2e", "p2f"):
            self.sems[name] = es.enter_context(nc.semaphore(name))
            self.cnt[name] = 0

    def step(self, eng, mark=None, inc=1):
        ev = None
        if mark is not None:
            self.cnt[mark] += inc
            ev = (mark, self.cnt[mark])

        def deco(f):
            sem = self.sems[mark] if mark is not None else None

            def run(e, _f=f, _sem=sem, _inc=inc):
                inst = _f(e)
                if _sem is not None:
                    inst.then_inc(_sem, _inc)

            self.prog[eng].append(run)
            return ev

        return deco

    def wait(self, eng, *evs):
        for ev in evs:
            if ev is None:
                continue
            name, val = ev
            if val <= 0:
                continue
            sem = self.sems[name]
            self.prog[eng].append(lambda e, _s=sem, _v=val: e.wait_ge(_s, _v))

    def ev(self, mark):
        return (mark, self.cnt[mark])


def build(stage="full"):
    nc = bass.Bass("TRN2", target_bir_lowering=False, debug=False)
    ap = lambda t: t.ap()

    x_h = ap(nc.dram_tensor("x", [T, C], F32, kind="ExternalInput"))
    xT_h = ap(nc.dram_tensor("xT", [C, T], F32, kind="ExternalInput"))
    Wla_h = ap(nc.dram_tensor("Wla", [C, C], F32, kind="ExternalInput"))
    Wv_h = ap(nc.dram_tensor("Wv", [C, C], F32, kind="ExternalInput"))
    Wp_h = ap(nc.dram_tensor("Wp", [C, C], F32, kind="ExternalInput"))
    W1_h = ap(nc.dram_tensor("W1", [4096, C], F32, kind="ExternalInput"))
    W2T_h = ap(nc.dram_tensor("W2T", [4096, C], F32, kind="ExternalInput"))
    b1_h = ap(nc.dram_tensor("b1rs", [128, 32], F32, kind="ExternalInput"))
    b2_h = ap(nc.dram_tensor("b2row", [1, C], F32, kind="ExternalInput"))
    acol_h = ap(nc.dram_tensor("acol", [128, NG], F32, kind="ExternalInput"))
    bcol_h = ap(nc.dram_tensor("bcol", [128, NH], F32, kind="ExternalInput"))
    vbcol_h = ap(nc.dram_tensor("vbcol", [128, NG], F32, kind="ExternalInput"))
    cbc_h = ap(nc.dram_tensor("cbc", [128, C], F32, kind="ExternalInput"))

    out_h = ap(nc.dram_tensor("out", [T, C], F32, kind="ExternalOutput"))
    ktm_h = ap(nc.dram_tensor("ktm_s", [NG, T, 128], BF16,
                              kind="ExternalOutput" if stage in ("k", "kv") else "Internal"))
    v65k = "ExternalOutput" if stage == "kv" else "Internal"
    x1f_kind = "ExternalOutput" if stage == "attn" else "Internal"
    x1T_h = ap(nc.dram_tensor("x1T_s", [C, T], BF16,
                              kind="ExternalOutput" if stage == "attn" else "Internal"))
    x1f_h = ap(nc.dram_tensor("x1f_s", [T, C], F32, kind=x1f_kind))
    if stage == "kv":
        v65d_h = ap(nc.dram_tensor("v65d", [128, NT * NH * 65], BF16, kind="ExternalOutput"))

    tri_strict_np = (np.arange(128)[:, None] < np.arange(128)[None, :]).astype(ml_dtypes.bfloat16)
    tri0_np = tri_strict_np.copy()
    tri0_np[0, 0] = 1
    tri_s_h = nc.inline_tensor(tri_strict_np, "tri_s").ap()
    tri0_h = nc.inline_tensor(tri0_np, "tri0").ap()
    ones64_h = nc.inline_tensor(np.ones((1, 64), ml_dtypes.bfloat16), "ones64c").ap()
    ones128_h = nc.inline_tensor(np.ones((1, 128), ml_dtypes.bfloat16), "ones128c").ap()
    ident_h = nc.inline_tensor(np.eye(128, dtype=ml_dtypes.bfloat16), "identc").ap()

    es = contextlib.ExitStack()
    E = es.enter_context
    SB = lambda n, s, d: E(nc.sbuf_tensor(n, s, d))
    cx = Ctx(nc, es)
    st, wt = cx.step, cx.wait

    psA = E(nc.psum_tensor("psA", [128, 1024], F32))
    psB = E(nc.psum_tensor("psB", [128, 1024], F32))
    psY = [E(nc.psum_tensor(f"psY{i}", [128, 512], F32)) for i in range(2)]
    psR = E(nc.psum_tensor("psR", [128, 512], F32))
    psP = E(nc.psum_tensor("psP", [128, 512], F32))

    acol = SB("acol_sb", [128, NG], F32)
    amc = SB("amc_sb", [128, NG], F32)
    kb2 = SB("kb2_sb", [128, NH], F32)
    nkb2 = SB("nkb2_sb", [128, NH], F32)
    vbc = SB("vbc_sb", [128, NG], F32)
    tri_s = SB("tri_s_sb", [128, 128], BF16)
    tri0 = SB("tri0_sb", [128, 128], BF16)
    ones64 = SB("ones64_sb", [1, 64], BF16)
    ones128 = SB("ones128_sb", [1, 128], BF16)
    ident = SB("ident_sb", [128, 128], BF16)

    for dst, src in [(acol, acol_h), (vbc, vbcol_h), (tri_s, tri_s_h),
                     (tri0, tri0_h), (ones64, ones64_h), (ones128, ones128_h),
                     (ident, ident_h), (kb2, bcol_h)]:
        st("sync", "qs", 16)(lambda e, d=dst, s=src: e.dma_start(out=d[:], in_=s))
    ev_small = cx.ev("qs")
    wt("vector", ev_small)
    st("vector", "dve")(lambda e: e.tensor_scalar(amc[:], acol[:], -1.0, 1.0, ALU.mult, ALU.add))
    st("vector", "dve")(lambda e: e.tensor_scalar(nkb2[:], kb2[:], 10.0, 5.0, ALU.mult, ALU.min))
    ev_coefv = cx.ev("dve")
    wt("scalar", ev_coefv)
    st("scalar", "act")(lambda e: e.activation(kb2[:], nkb2[:], AF.Exp, bias=0.0, scale=2.0))
    st("scalar", "act")(lambda e: e.activation(vbc[:], vbc[:], AF.Exp, bias=0.0, scale=10.0))
    ev_coefs = cx.ev("act")
    wt("vector", ev_coefs)
    st("vector", "dve")(lambda e: e.tensor_scalar_mul(nkb2[:], kb2[:], -1.0))
    ev_nkb2 = cx.ev("dve")

    es_v65 = es.enter_context(contextlib.ExitStack())
    v65 = es_v65.enter_context(nc.sbuf_tensor("v65_sb", [128, NT * NH * 65 + 64], BF16))

    es_xt = es.enter_context(contextlib.ExitStack())
    SBx = lambda n, s, d: es_xt.enter_context(nc.sbuf_tensor(n, s, d))
    xT_bf = SBx("xT_bf", [128, KC * XTP], BF16)

    ev_xtk = {}
    KCPOOL = ("p2a", "p2b", "p2c", "p2d", "p2e", "p2f", "px0", "px1")
    for kc in range(KC):
        st("gpsimd", KCPOOL[kc], 16)(lambda e, kc=kc: e.dma_start(
            out=xT_bf[:, kc * XTP:kc * XTP + T],
            in_=xT_h[128 * kc:128 * (kc + 1), :]))
    for kc in range(KC):
        ev_xtk[kc] = st("gpsimd", KCPOOL[kc], 16)(lambda e, kc=kc: e.dma_start(
            out=Wla_bf[:, kc * C:(kc + 1) * C],
            in_=Wla_h[128 * kc:128 * (kc + 1), :]))
    ev_xt = tuple(ev_xtk[kc] for kc in range(KC))
    wt("vector", *ev_xt)
    st("vector", "dve")(lambda e: e.memset(
        xT_bf[:].rearrange("p (k c) -> p k c", c=XTP)[:, :, T:XTP], 0.0))
    ev_xtpad = cx.ev("dve")

    es_k = es.enter_context(contextlib.ExitStack())
    SBk = lambda n, s, d: es_k.enter_context(nc.sbuf_tensor(n, s, d))
    Wla_bf = SBk("Wla_bf", [128, KC * C], BF16)
    xls = SBk("xls", [128, T], F32)
    krbf = [SBk(f"krbf{i}", [128, T], BF16) for i in range(2)]
    ktm = [SBk(f"ktm{i}", [128, T], BF16) for i in range(2)]
    ksq = SBk("ksq", [128, T], BF16)
    kss = SBk("kss", [128, 64], F32)

    wt("tensor", ev_xtpad)
    ev_scan = {}
    ev_spill = {}
    ev_tposed = {}
    ev_evac = {}
    ps_free = {}
    tp_banks = [psY[0], psY[1], psR, psP]
    ev_tevac = {}
    ev_tpe = {}

    def emit_trip1_and_norm(g):
        bi = g % 2
        wt("tensor", ev_scan[g])
        for bank in range(4):
            if g >= 1:
                wt("tensor", ev_tevac[(g - 1, bank)])
            tpe = None
            for j8 in range(8):
                j = bank * 8 + j8
                tpe = st("tensor", "pe")(lambda e, j=j, j8=j8, bi=bi, bank=bank: e.transpose(
                    tp_banks[bank][:].bitcast(BF16)[:, j8 * 128:(j8 + 1) * 128],
                    krbf[bi][:, 128 * j:128 * (j + 1)], ident[:]))
            wt("vector", tpe)
            if g >= 2:
                wt("vector", ev_spill[g - 2])
            ev_tevac[(g, bank)] = st("vector", "dve")(lambda e, bank=bank, bi=bi: e.tensor_copy(
                ktm[bi][:, bank * 1024:(bank + 1) * 1024], tp_banks[bank][:].bitcast(BF16)))
        ev_tpe[g] = tpe
        wt("scalar", ev_tevac[(g, 3)])
        ev_sq = st("scalar", "act")(lambda e, bi=bi: e.activation(ksq[:], ktm[bi][:], AF.Square))
        wt("vector", ev_sq)
        ev_red = st("vector", "dve")(lambda e: e.reduce_sum(
            kss[:], ksq[:].rearrange("p (j h c) -> p j h c", h=2, c=64), axis=AX.X))
        wt("scalar", ev_red)
        st("scalar", "act")(lambda e: e.activation(kss[:], kss[:], AF.Ln))
        ev_ex = st("scalar", "act")(lambda e: e.activation(kss[:], kss[:], AF.Exp, scale=-0.5))
        wt("vector", ev_ex)
        for col in range(64):
            st("vector", "dve" if col == 63 else None)(lambda e, col=col, bi=bi: e.tensor_scalar(
                ktm[bi][:, col * 64:(col + 1) * 64], ktm[bi][:, col * 64:(col + 1) * 64],
                kss[:, col:col + 1], None, ALU.mult))
        ev_appl = cx.ev("dve")
        wt("sync", ev_appl)
        ev_spill[g] = st("sync", "pk%d" % bi, 16)(lambda e, g=g, bi=bi: e.dma_start(
            out=ktm_h[g].rearrange("(j q) c -> q j c", q=128),
            in_=ktm[bi][:].rearrange("p (j c) -> p j c", c=128)))

    for g in range(NG):
        bi = g % 2
        for tc in range(8):
            slot = (g * 8 + tc) % 4
            pst = psA if slot < 2 else psB
            half = (slot % 2) * 512
            if slot in ps_free:
                wt("tensor", ps_free[slot])
            ev = None
            for kc in range(KC):
                if g == 0 and tc == 0:
                    wt("tensor", ev_xtk[kc])
                ev = st("tensor", "pe" if kc == KC - 1 else None)(
                    lambda e, kc=kc, g=g, tc=tc, pst=pst, half=half: e.matmul(
                        pst[:, half:half + 512],
                        Wla_bf[:, kc * C + g * 128: kc * C + (g + 1) * 128],
                        xT_bf[:, kc * XTP + tc * 512: kc * XTP + (tc + 1) * 512],
                        start=(kc == 0), stop=(kc == KC - 1)))
            wt("vector", ev)
            ps_free[slot] = st("vector", "dve")(
                lambda e, g=g, tc=tc, pst=pst, half=half: e.tensor_scalar(
                    xls[:, tc * 512:(tc + 1) * 512], pst[:, half:half + 512],
                    amc[:, g:g + 1], None, ALU.mult))
        ev_evac[g] = cx.ev("dve")
        if g >= 2:
            wt("vector", ev_tpe[g - 2])
        ev_scan[g] = st("vector", "dve")(lambda e, bi=bi, g=g: e.tensor_tensor_scan(
            krbf[bi][:], acol[:, g:g + 1].broadcast_to([128, T]), xls[:],
            0.0, ALU.mult, ALU.add))
        if g >= 1:
            emit_trip1_and_norm(g - 1)
    emit_trip1_and_norm(NG - 1)

    es_k.close()

    es_v = es.enter_context(contextlib.ExitStack())
    SBv = lambda n, s, d: es_v.enter_context(nc.sbuf_tensor(n, s, d))
    ev_k_done = (cx.ev("dve"), ev_spill[NG - 1])
    Wvc_bf = SBv("Wvc_bf", [128, KC * C], BF16)
    Wv1_bf = SBv("Wv1_bf", [128, KC * C], BF16)
    wtmp = SBv("wtmp", [128, C], F32)
    vsq = [SBv(f"vsq{i}", [128, C], F32) for i in range(2)]
    vss = [SBv(f"vss{i}", [128, NH], F32) for i in range(2)]
    vsb = [SBv(f"vsb{i}", [128, C], F32) for i in range(2)]
    cb = SBv("cb_sb", [128, C], F32)
    cmb = SBv("cmb_sb", [128, C], F32)
    wt("sync", ev_k_done[0])
    st("sync", "qs", 16)(lambda e: e.dma_start(out=cb[:], in_=cbc_h))
    ev_cbl = cx.ev("qs")
    wt("vector", ev_cbl)
    st("vector", "dve")(lambda e: e.tensor_scalar(cmb[:], cb[:], -1.0, 1.0, ALU.mult, ALU.add))

    ev_wvd = {}
    for kc in range(KC):
        wt("sync", ev_k_done[0] if kc == 0 else ev_wvd[kc - 1])
        st("sync", "qs", 16)(lambda e, kc=kc: e.dma_start(out=wtmp[:], in_=Wv_h[128 * kc:128 * (kc + 1), :]))
        ev_l = cx.ev("qs")
        wt("vector", ev_l)
        st("vector", "dve")(lambda e, kc=kc: e.tensor_mul(Wvc_bf[:, kc * C:(kc + 1) * C], wtmp[:], cb[:]))
        ev_wvd[kc] = st("vector", "dve")(lambda e, kc=kc: e.tensor_mul(Wv1_bf[:, kc * C:(kc + 1) * C], wtmp[:], cmb[:]))
    st("vector", "dve")(lambda e: e.memset(v65[:], 1.0))
    ev_v65init = cx.ev("dve")

    ev_vdone = {}
    wt("tensor", ev_wvd[KC - 1])
    for i in range(NT):
        bi = i % 2
        pst = psA if i % 2 == 0 else psB
        if i >= 2:
            wt("tensor", ev_vdone[i - 2])
        ev = None
        first = True
        for half in range(2):
            for kc in range(KC):
                for (w, sh) in ((Wvc_bf, 0), (Wv1_bf, 1)):
                    last = (half == 1 and kc == KC - 1 and sh == 1)
                    ev = st("tensor", "pe" if last else None)(
                        lambda e, kc=kc, w=w, sh=sh, half=half, i=i, pst=pst, first=first,
                               lst=(kc == KC - 1 and sh == 1): e.matmul(
                            pst[:, half * 512:half * 512 + 512],
                            xT_bf[:, kc * XTP + i * 128 + sh: kc * XTP + i * 128 + sh + 128],
                            w[:, kc * C + half * 512: kc * C + half * 512 + 512],
                            start=first, stop=lst))
                    first = False
            first = True
        wt("scalar", ev)
        ev_sq = st("scalar", "act")(lambda e, bi=bi, pst=pst: e.activation(vsq[bi][:], pst[:], AF.Square))
        wt("vector", ev_sq)
        ev_red = st("vector", "dve")(lambda e, bi=bi: e.reduce_sum(
            vss[bi][:], vsq[bi][:].rearrange("p (h c) -> p h c", c=64), axis=AX.X))
        wt("scalar", ev_red)
        st("scalar", "act")(lambda e, bi=bi: e.activation(vss[bi][:], vss[bi][:], AF.Ln))
        ev_ex = st("scalar", "act")(lambda e, bi=bi: e.activation(vss[bi][:], vss[bi][:], AF.Exp, scale=-0.5))
        wt("vector", ev_ex)
        st("vector", "dve")(lambda e, bi=bi: e.tensor_copy(
            vsb[bi][:].rearrange("p (h c) -> p h c", c=64),
            vss[bi][:].rearrange("p (h c) -> p h c", c=1).broadcast_to([128, NH, 64])))
        if i == 0:
            wt("vector", ev_v65init)
        ev_vdone[i] = st("vector", "dve")(lambda e, bi=bi, i=i, pst=pst: e.tensor_mul(
            v65[:, i * NH * 65:(i + 1) * NH * 65].rearrange("p (h c) -> p h c", c=65)[:, :, 0:64],
            pst[:].rearrange("p (h c) -> p h c", c=64),
            vsb[bi][:].rearrange("p (h c) -> p h c", c=64)))

    if stage == "kv":
        wt("sync", ev_vdone[NT - 1])
        st("sync", "qs", 16)(lambda e: e.dma_start(out=v65d_h, in_=v65[:]))
        ev_fin = cx.ev("qs")
        wt("sync", ev_fin)
        es_v.close(); es_xt.close()
        finish(cx)
        return nc
    es_v.close(); es_xt.close()
    if stage == "k":
        wt("sync", ev_spill[NG - 1])
        finish(cx)
        return nc

    es_c = es.enter_context(contextlib.ExitStack())
    SBc = lambda n, s, d: es_c.enter_context(nc.sbuf_tensor(n, s, d))
    kT = SBc("kT_sb", [128, NG * T], BF16)
    Wp_bf = SBc("Wp_bf", [128, KC * C], BF16)
    es_w = contextlib.ExitStack()
    wtmp2 = es_w.enter_context(nc.sbuf_tensor("wtmp2", [128, C], F32))
    P_sb = None

    ev_1a_done = (cx.ev("dve"), cx.ev("pe"), cx.ev("act"))

    ev_kld = {}
    wt("sync", ev_1a_done[0], ev_1a_done[1])

    def reload_wave(gq):
        for kj in range(4 * gq, 4 * gq + 4):
            for g in range(NG):
                st("sync", "qt", 16)(lambda e, g=g, kj=kj: e.dma_start_transpose(
                    kT[:, g * T + 128 * kj: g * T + 128 * (kj + 1)],
                    ktm_h[g, 128 * kj:128 * (kj + 1), :]))
        ev_kld[gq] = (cx.ev("qt"),)

    wt("sync", ev_spill[NG - 1], ev_spill[NG - 2])
    reload_wave(0)
    reload_wave(1)

    ev_wpd = {}
    for gc in range(KC):
        if gc > 0:
            wt("sync", ev_wpd[gc - 1])
        st("sync", "qs", 16)(lambda e, gc=gc: e.dma_start(out=wtmp2[:], in_=Wp_h[128 * gc:128 * (gc + 1), :]))
        ev_l = cx.ev("qs")
        wt("vector", ev_l)
        ev_wpd[gc] = st("vector", "dve")(lambda e, gc=gc: e.tensor_scalar(
            Wp_bf[:, gc * C:(gc + 1) * C], wtmp2[:], vbc[:, gc:gc + 1], None, ALU.mult))
    es_w.close()
    P_sb = [SBc(f"P{i}", [128, 1536], BF16) for i in range(2)]
    ynT = [SBc(f"ynT{i}", [128, NG * 512], BF16) for i in range(2)]
    rrow = SBc("rrow", [1, 512], F32)
    rhi = [SBc(f"rhi{i}", [1, 512], BF16) for i in range(2)]
    rlo = [SBc(f"rlo{i}", [1, 512], BF16) for i in range(2)]
    rbc = SBc("rbc", [64, 512], F32)
    yst = [SBc(f"yst{i}", [64, 512], BF16) for i in range(2)]
    lrow = SBc("lrow2", [1, 1024], F32)
    x_sb = [SBc(f"x_sb{i}", [128, C], F32) for i in range(2)]
    h_sb = [SBc(f"h_sb{i}", [128, C], F32) for i in range(2)]
    stat = SBc("stat", [128, 8], F32)
    x1f_sb = SBc("x1f", [128, C], F32)
    x1b_sb = SBc("x1b", [128, C], BF16)
    x1stage = SBc("x1stage", [128, C], BF16)

    x_ev = {}

    def load_x(ti):
        wt("sync", ev_1a_done[0])
        return st("sync", "px%d" % (ti % 2), 16)(lambda e, ti=ti: e.dma_start(
            out=x_sb[ti % 2][:], in_=x_h[128 * ti:128 * (ti + 1), :]))

    x_ev[0] = load_x(0)
    x_ev[1] = load_x(1)

    gS = 0
    pend_lpath = []
    pend_lsc = []
    pend_pv = []
    pend_proj = []
    rr_par = lambda hi_: hi_ % 2
    rr_t = lambda hb: (rrow, rhi[hb], rlo[hb])

    def emit_pv(key, kj, pbuf, h_, hi_, ypar_, nkj_, gq_, slot, off, mev):
        if kj == 0 and hi_ >= 2:
            wt("tensor", ev_yev[hi_ - 2])
        wt("tensor", ev_expg[key])
        if mev is not None:
            wt("tensor", mev)
        pvev = st("tensor", "pe")(
            lambda e, kj=kj, h=h_, off=off, pbuf=pbuf, ypar=ypar_, ps3=kj % 3,
                   fi=(kj == 0), la=(kj == nkj_ - 1): e.matmul(
                psY[ypar][:, off:512],
                v65[:, kj * NH * 65 + h * 65: kj * NH * 65 + h * 65 + 128],
                pbuf[:, ps3 * 512 + off: (ps3 + 1) * 512],
                start=fi, stop=la))
        ev_pvg[key] = pvev
        return pvev

    def emit_lsc(hi_, g_, e64_, par_, ypar_, nkj_):
        hb = rr_par(hi_)
        wt("scalar", ev_yev[hi_])
        if hi_ >= 1:
            wt("scalar", ev_rlo_d[hi_ - 1])
        st("scalar", "act")(lambda e, hb=hb: e.activation(
            rr_t(hb)[0][:], lrow[0:1, hb * 512:hb * 512 + 512], AF.Ln))
        ev_r = st("scalar", "act")(lambda e, hb=hb: e.activation(
            rr_t(hb)[0][:], rr_t(hb)[0][:], AF.Exp, scale=-1.0))
        ev_expr[hi_] = ev_r
        wt("vector", ev_r)
        wt("vector", ev_lbcd.get(hi_ - 2))
        st("vector", "dve")(lambda e, hb=hb: e.tensor_copy(rr_t(hb)[1][:], rr_t(hb)[0][:]))
        ev_rlo_d[hi_] = st("vector", "dve")(lambda e, hb=hb: e.tensor_sub(
            rr_t(hb)[2][:], rr_t(hb)[0][:], rr_t(hb)[1][:]))
        pend_lpath.append((hi_, g_, e64_, par_, ypar_))

    def emit_lpath_te(hi_, g_, e64_, par_, ypar_):
        hb = rr_par(hi_)
        wt("tensor", ev_rlo_d[hi_])
        if hi_ >= 1:
            wt("tensor", ev_ynorm[hi_ - 1])
        if (hi_ % NH) == 0 and hi_ >= NH:
            wt("tensor", ev_lnv.get(4 * (hi_ // NH) - 1))
        st("tensor", None)(lambda e, hb=hb: e.matmul(psR[0:64, :], ones64[:], rr_t(hb)[1][:], start=True, stop=False))
        ev_lbc = st("tensor", "pe")(lambda e, hb=hb: e.matmul(psR[0:64, :], ones64[:], rr_t(hb)[2][:], start=False, stop=True))
        ev_lbcd[hi_] = ev_lbc
        wt("vector", ev_lbc)
        st("vector", "dve")(lambda e: e.tensor_copy(rbc[:], psR[0:64, :]))
        if (hi_ % NH) == 0 and hi_ // NH >= 2:
            wt("vector", ev_projmm[4 * (hi_ // NH - 2) + 3])
        ev_ynorm[hi_] = st("vector", "dve")(lambda e, g=g_, e64=e64_, par=par_, hb=rr_par(hi_): e.tensor_mul(
            ynT[par][e64:e64 + 64, g * 512:(g + 1) * 512], yst[hb][:], rbc[:]))

    def emit_projgq(gq, par, ev_yn_all):
        for qt in range(4):
            ti = 4 * gq + qt
            bi = ti % 2
            wt("tensor", ev_yn_all)
            if ti >= 1:
                wt("tensor", ev_lnv.get(ti - 1))
            pe_ev = None
            for half, pp in ((0, psP), (1, psR)):
                for gc in range(KC):
                    pe_ev = st("tensor", "pe" if gc == KC - 1 else None)(
                        lambda e, gc=gc, half=half, qt=qt, par=par, pp=pp: e.matmul(
                            pp[:, 0:512],
                            ynT[par][:, gc * 512 + qt * 128: gc * 512 + (qt + 1) * 128],
                            Wp_bf[:, gc * C + half * 512: gc * C + half * 512 + 512],
                            start=(gc == 0), stop=(gc == KC - 1)))
            ev_projmm[ti] = pe_ev
            wt("vector", pe_ev, x_ev[ti])
            st("vector", "dve")(lambda e, bi=bi: e.scalar_tensor_tensor(
                h_sb[bi][:, 0:512], psP[:, 0:512], 1.0, x_sb[bi][:, 0:512],
                ALU.mult, ALU.add, accum_out=stat[:, 0:1]))
            st("vector", "dve")(lambda e, bi=bi: e.scalar_tensor_tensor(
                h_sb[bi][:, 512:1024], psR[:, 0:512], 1.0, x_sb[bi][:, 512:1024],
                ALU.mult, ALU.add, accum_out=stat[:, 1:2]))
            if ti >= 1:
                wt("vector", ev_st_spl[ti - 1])
            st("vector", "dve")(lambda e, bi=bi: e.scalar_tensor_tensor(
                x1stage[:], h_sb[bi][:], 1.0, h_sb[bi][:], ALU.mult, ALU.mult,
                accum_out=stat[:, 2:3]))
            e1 = st("vector", "dve")(lambda e: e.tensor_add(stat[:, 0:1], stat[:, 0:1], stat[:, 1:2]))
            wt("vector", e1)
            e2 = st("vector", "dve")(lambda e: e.tensor_scalar_mul(stat[:, 3:4], stat[:, 0:1], -1.0 / C))
            wt("vector", e2)
            e3 = st("vector", "dve")(lambda e: e.tensor_mul(stat[:, 4:5], stat[:, 3:4], stat[:, 3:4]))
            wt("vector", e3)
            e4 = st("vector", "dve")(lambda e: e.scalar_tensor_tensor(
                stat[:, 5:6], stat[:, 2:3], 1.0 / C, stat[:, 4:5], ALU.mult, ALU.subtract))
            wt("vector", e4)
            ev_var = st("vector", "dve")(lambda e: e.tensor_scalar_add(stat[:, 5:6], stat[:, 5:6], EPS))
            wt("scalar", ev_var)
            st("scalar", "act")(lambda e: e.activation(stat[:, 6:7], stat[:, 5:6], AF.Ln))
            ev_rstd = st("scalar", "act")(lambda e: e.activation(stat[:, 6:7], stat[:, 6:7], AF.Exp, scale=-0.5))
            wt("vector", ev_rstd)
            if ti >= 1:
                wt("vector", ev_spl_f[ti - 1])
            ev_x1 = st("vector", "dve")(lambda e, bi=bi: e.tensor_scalar(
                x1f_sb[:], h_sb[bi][:], stat[:, 3:4], stat[:, 6:7], ALU.add, ALU.mult))
            ev_lnv[ti] = ev_x1
            if ti >= 1:
                wt("vector", ev_t_spl[ti - 1])
            ev_x1b = st("vector", "dve")(lambda e: e.tensor_copy(x1b_sb[:], x1f_sb[:]))
            wt("sync", ev_x1b)
            ev_spl_f[ti] = st("sync", "pf%d" % (ti % 2), 16)(lambda e, ti=ti: e.dma_start(
                out=x1f_h[128 * ti:128 * (ti + 1), :], in_=x1f_sb[:]))
            if ti >= 1:
                wt("sync", ev_st_spl[ti - 1])
            for gc in range(NG):
                st("sync", "qt", 16)(lambda e, gc=gc: e.dma_start_transpose(
                    x1stage[:, gc * 128:(gc + 1) * 128], x1b_sb[:, gc * 128:(gc + 1) * 128]))
            ev_t_spl[ti] = cx.ev("qt")
            wt("sync", ev_t_spl[ti])
            ev_st_spl[ti] = st("sync", "pst", 16)(lambda e, ti=ti: e.dma_start(
                out=x1T_h.rearrange("(gc p) t -> p gc t", p=128)[:, :, 128 * ti:128 * (ti + 1)],
                in_=x1stage[:].rearrange("p (gc c) -> p gc c", c=128)))
            if ti + 2 < NT:
                wt("sync", ev_x1)
                x_ev[ti + 2] = st("sync", "px%d" % (ti % 2), 16)(lambda e, ti=ti: e.dma_start(
                    out=x_sb[ti % 2][:], in_=x_h[128 * (ti + 2):128 * (ti + 3), :]))

    ev_expg = {}
    ev_smm = {}
    ev_pvg = {}
    ev_ynorm = {}
    ev_projmm = {}
    ev_lnv = {}
    ev_rlo_d = {}
    ev_yev = {}
    ev_expr = {}
    ev_lbcd = {}
    ev_spl_f = {}
    ev_t_spl = {}
    ev_st_spl = {}
    ev_ost = {}
    hi = 0
    for gq in range(QG):
        nkj = 4 * gq + 4
        par = gq % 2
        if gq + 2 < QG:
            reload_wave(gq + 2)
        for hp in range(NH // 2):
            g = hp
            nb = (nkj + 1) // 2
            if hp == 1 and pend_proj:
                emit_projgq(*pend_proj.pop(0))
            if hi == 0:
                wt("scalar", ev_nkb2)
            for kj in range(nkj):
                slot = kj % 2
                off = 128 * max(0, kj - 4 * gq)
                for side, e64, pst, pbuf in ((0, 0, psA, P_sb[0]), (1, 64, psB, P_sb[1])):
                    hix = hi + side
                    if kj >= 2:
                        wt("tensor", ev_expg[(hix, kj - 2)])
                    if hi == 0 and kj == 0 and side == 0:
                        wt("tensor", ev_kld[gq][0], ev_wpd[KC - 1])
                    elif kj == 0 and hp == 0 and side == 0:
                        wt("tensor", ev_kld[gq][0])
                    ev_smm[(hix, kj)] = st("tensor", "pe")(
                        lambda e, g=g, e64=e64, kj=kj, gq=gq, off=off, pst=pst, slot=slot: e.matmul(
                            pst[:, slot * 512 + off: (slot + 1) * 512],
                            kT[e64:e64 + 64, g * T + 128 * kj: g * T + 128 * (kj + 1)],
                            kT[e64:e64 + 64, g * T + 512 * gq + off: g * T + 512 * (gq + 1)],
                            start=True, stop=True, tile_position=(e64, 0)))
                if len(pend_pv) >= 2:
                    emit_pv(*pend_pv.pop(0))
                    emit_pv(*pend_pv.pop(0))
                for side, e64, pst, pbuf in ((0, 0, psA, P_sb[0]), (1, 64, psB, P_sb[1])):
                    h = 2 * hp + side
                    hix = hi + side
                    wt("scalar", ev_smm[(hix, kj)])
                    if kj >= 3:
                        wt("scalar", ev_pvg[(hix, kj - 3)])
                    ps3 = kj % 3
                    ev_expg[(hix, kj)] = st("scalar", "act")(
                        lambda e, pst=pst, pbuf=pbuf, h=h, slot=slot, off=off, ps3=ps3: e.activation(
                            pbuf[:, ps3 * 512 + off: (ps3 + 1) * 512],
                            pst[:, slot * 512 + off: (slot + 1) * 512], AF.Exp,
                            bias=nkb2[:, h:h + 1], scale=kb2[:, h:h + 1]))
                    mev = None
                    if kj >= 4 * gq:
                        tri = tri0 if (gq == 0 and kj == 0) else tri_s
                        wt("gpsimd", ev_expg[(hix, kj)])
                        mev = st("gpsimd", "pool")(lambda e, pbuf=pbuf, ps3=kj % 3, off=off, tri=tri: e.tensor_mul(
                            pbuf[:, ps3 * 512 + off: ps3 * 512 + off + 128],
                            pbuf[:, ps3 * 512 + off: ps3 * 512 + off + 128], tri[:]))
                    pend_pv.append(((hix, kj), kj, pbuf, h, hix, side, nkj, gq, slot, off, mev))
                    if side == 1 and kj in (1, 2) and pend_lsc:
                        emit_lsc(*pend_lsc.pop(0))
                    if side == 1 and kj in (3, 4) and pend_lpath:
                        emit_lpath_te(*pend_lpath.pop(0))
            while pend_pv:
                pvev = emit_pv(*pend_pv.pop(0))
            while pend_lpath:
                emit_lpath_te(*pend_lpath.pop(0))
            for side in range(2):
                hix = hi + side
                hb = rr_par(hix)
                wt("vector", ev_pvg[(hix, nkj - 1)])
                if hix >= 2:
                    wt("vector", ev_expr.get(hix - 2))
                st("vector", "dve")(lambda e, ypar=side, hb=hb: e.tensor_copy(
                    yst[hb][:], psY[ypar][0:64, :]))
                ev_yev[hix] = st("vector", "dve")(lambda e, ypar=side, hb=hb: e.tensor_copy(
                    lrow[0:1, hb * 512:hb * 512 + 512], psY[ypar][64:65, :]))
                pend_lsc.append((hix, g, side * 64, par, side, nkj))
            hi += 2
        while pend_lsc:
            emit_lsc(*pend_lsc.pop(0))
        while pend_lpath:
            emit_lpath_te(*pend_lpath.pop(0))
        if gq == 0:
            wt("vector", ev_ynorm[hi - 1])
            for gc in range(NG):
                st("vector", "dve")(lambda e, gc=gc: e.memset(ynT[0][:, gc * 512:gc * 512 + 1], 0.0))
        ev_yn_all = cx.ev("dve")
        pend_proj.append((gq, par, ev_yn_all))

    while pend_proj:
        emit_projgq(*pend_proj.pop(0))

    ev_1c_spill_done = cx.ev("qs")
    ev_1c_pools = (cx.ev("pf0"), cx.ev("pf1"), cx.ev("pst"), cx.ev("px0"), cx.ev("px1"))
    if stage == "attn":
        wt("sync", ev_1c_spill_done)
        es_c.close()
        finish(cx)
        return nc
    es_c.close()
    es_v65.close()

    es_2 = es.enter_context(contextlib.ExitStack())
    SB2 = lambda n, s, d: es_2.enter_context(nc.sbuf_tensor(n, s, d))
    W2T_bf = SB2("W2T_bf", [128, 32 * C], BF16)
    w1s = [SB2(f"w1s{i}", [128, 32 * 128], BF16) for i in range(2)]
    FF_bf = SB2("FF_bf", [128, KC * C], BF16)
    bias2 = SB2("bias2", [1, C], BF16)
    b1_bf = SB2("b1_bf", [128, 32], BF16)
    x1t_sb = [SB2(f"x1t{i}", [128, NG * 128], BF16) for i in range(3)]
    b2row = SB2("b2row_sb", [1, C], F32)
    x2f_sb = [SB2(f"x2f{i}", [128, C], F32) for i in range(3)]
    h2_sb = [SB2(f"h2_sb{i}", [128, C], F32) for i in range(2)]
    h2scr = SB2("h2scr", [128, C], BF16)
    st2 = SB2("st2", [128, 8], F32)
    o_sb = [SB2(f"o_sb{i}", [128, C], F32) for i in range(2)]

    ev_1c_done = (cx.ev("pe"), cx.ev("dve"), cx.ev("act"), ev_1c_spill_done)
    wt("sync", *ev_1c_done[:3])
    st("sync", "qs", 16)(lambda e: e.dma_start(out=b2row[:], in_=b2_h))
    ev_b2l = cx.ev("qs")
    wt("gpsimd", *ev_1c_done[:3])
    wstg = [SB2(f"wstg{i}", [128, 4 * C], F32) for i in range(2)]
    ev_w2c = {}
    for q4 in range(8):
        if q4 >= 2:
            wt("sync", ev_w2c[q4 - 2])
        st("sync", ("pf0", "pf1")[q4 % 2], 16)(lambda e, q4=q4: e.dma_start(
            out=wstg[q4 % 2][:].rearrange("p (oc c) -> p oc c", c=C),
            in_=W2T_h[q4 * 512:(q4 + 1) * 512, :].rearrange("(oc p) c -> p oc c", p=128)))
        evl = cx.ev(("pf0", "pf1")[q4 % 2])
        wt("vector", evl)
        ev_w2c[q4] = st("vector", "dve")(lambda e, q4=q4: e.tensor_copy(
            W2T_bf[:, q4 * 4 * C:(q4 + 1) * 4 * C], wstg[q4 % 2][:]))
    ev_w2 = ev_w2c[7]
    st("gpsimd", "qg", 16)(lambda e: e.dma_start(out=b1_bf[:], in_=b1_h))
    ev_b1 = cx.ev("qg")
    wt("tensor", ev_w2, ev_b1)
    for half, pb in ((0, psY[0]), (1, psY[1])):
        for oc in range(32):
            st("tensor", "pe" if oc == 31 else None)(lambda e, oc=oc, half=half, pb=pb: e.matmul(
                pb[0:1, :], b1_bf[:, oc:oc + 1],
                W2T_bf[:, oc * C + half * 512: oc * C + half * 512 + 512],
                start=(oc == 0), stop=(oc == 31)))
    ev_b2mm = cx.ev("pe")
    wt("vector", ev_b2mm, ev_b2l)
    st("vector", "dve")(lambda e: e.tensor_add(bias2[0:1, 0:512], psY[0][0:1, :], b2row[0:1, 0:512]))
    ev_bias2 = st("vector", "dve")(lambda e: e.tensor_add(bias2[0:1, 512:1024], psY[1][0:1, :], b2row[0:1, 512:1024]))

    ev_ffd = {}
    ev_w1l = {}
    ev_ffmm = {}

    w1f = SB2("w1f", [128, 32 * 128], F32)

    def load_w1(hc):
        if hc >= 1:
            wt("sync", ev_w1l[hc - 1])
        for h2_ in range(2):
            st("sync", "qs", 16)(lambda e, hc=hc, h2_=h2_: e.dma_start(
                out=w1f[:, h2_ * 16 * 128:(h2_ + 1) * 16 * 128].rearrange("p (oc c) -> p oc c", c=128),
                in_=W1_h[h2_ * 2048:(h2_ + 1) * 2048, 128 * hc:128 * (hc + 1)].rearrange("(oc p) c -> p oc c", p=128)))
        evl = cx.ev("qs")
        wt("vector", evl)
        if hc >= 2:
            wt("vector", ev_ffmm[hc - 2])
        ev_w1l[hc] = st("vector", "dve")(lambda e, hc=hc: e.tensor_copy(w1s[hc % 2][:], w1f[:]))

    load_w1(0)
    load_w1(1)
    for hc in range(KC):
        wt("tensor", ev_w1l[hc])
        if hc >= 1:
            wt("tensor", ev_ffd[hc - 1])
        ev = None
        for half in range(2):
            for oc in range(32):
                ev = st("tensor", "pe" if (oc == 31 and half == 1) else None)(
                    lambda e, oc=oc, hc=hc, half=half: e.matmul(
                        psA[:, half * 512:half * 512 + 512],
                        w1s[hc % 2][:, oc * 128:(oc + 1) * 128],
                        W2T_bf[:, oc * C + half * 512: oc * C + half * 512 + 512],
                        start=(oc == 0), stop=(oc == 31)))
        ev_ffmm[hc] = ev
        wt("vector", ev)
        ev_ffd[hc] = st("vector", "dve")(lambda e, hc=hc: e.tensor_copy(FF_bf[:, hc * C:(hc + 1) * C], psA[:]))
        if hc + 2 < KC:
            load_w1(hc + 2)

    if stage == "ffa":
        ffd_h = ap(nc.dram_tensor("ffd", [128, KC * C], BF16, kind="ExternalOutput"))
        b2d_h = ap(nc.dram_tensor("b2d", [1, C], BF16, kind="ExternalOutput"))
        wt("sync", ev_ffd[KC - 1], ev_bias2)
        st("sync", "qs", 16)(lambda e: e.dma_start(out=ffd_h, in_=FF_bf[:]))
        st("sync", "qs", 16)(lambda e: e.dma_start(out=b2d_h, in_=bias2[:]))
        evf = cx.ev("qs")
        wt("sync", evf)
        es_2.close()
        finish(cx)
        return nc

    ev_x1t = {}
    ev_x1f = {}
    ev_oev = {}
    ev_ost2 = {}
    wt("sync", ev_1c_done[3], *ev_1c_pools)

    def load_tile2(ti):
        e1 = st("sync", ("p2a", "p2b", "p2c")[ti % 3], 16)(lambda e, ti=ti: e.dma_start(
            out=x1t_sb[ti % 3][:].rearrange("p (gc c) -> p gc c", c=128),
            in_=x1T_h.rearrange("(gc p) t -> p gc t", p=128)[:, :, 128 * ti:128 * (ti + 1)]))
        ev_x1t[ti] = (e1,)
        ev_x1f[ti] = st("sync", ("p2d", "p2e", "p2f")[ti % 3], 16)(lambda e, ti=ti: e.dma_start(
            out=x2f_sb[ti % 3][:], in_=x1f_h[128 * ti:128 * (ti + 1), :]))

    load_tile2(0)
    load_tile2(1)
    load_tile2(2)
    for ti in range(NT):
        bi = ti % 2
        pst2 = psB if ti % 2 == 0 else psA
        wt("tensor", ev_x1t[ti][0])
        if ti == 0:
            wt("tensor", ev_ffd[KC - 1])
        if ti >= 2:
            wt("tensor", ev_oev[ti - 2])
        if PARANOID and ti >= 1:
            wt("tensor", ev_oev[ti - 1])
            wt("tensor", ev_ost2.get(ti - 1))
        pe_ev = None
        for half in range(2):
            for gc in range(KC):
                pe_ev = st("tensor", None)(lambda e, gc=gc, half=half, ti=ti, bi=bi, pst2=pst2: e.matmul(
                    pst2[:, half * 512:half * 512 + 512],
                    x1t_sb[ti % 3][:, gc * 128:(gc + 1) * 128],
                    FF_bf[:, gc * C + half * 512: gc * C + half * 512 + 512],
                    start=(gc == 0), stop=False))
            pe_ev = st("tensor", "pe" if half == 1 else None)(
                lambda e, half=half, pst2=pst2: e.matmul(
                    pst2[:, half * 512:half * 512 + 512], ones128[:],
                    bias2[0:1, half * 512:half * 512 + 512], start=False, stop=True))
        if ti == 0:
            wt("vector", ev_bias2)
        wt("vector", pe_ev, ev_x1f[ti])
        st("vector", "dve")(lambda e, bi=bi, ti=ti, pst2=pst2: e.scalar_tensor_tensor(
            h2_sb[bi][:, 0:512], pst2[:, 0:512], 1.0, x2f_sb[ti % 3][:, 0:512],
            ALU.mult, ALU.add, accum_out=st2[:, 0:1]))
        st("vector", "dve")(lambda e, bi=bi, ti=ti, pst2=pst2: e.scalar_tensor_tensor(
            h2_sb[bi][:, 512:1024], pst2[:, 512:1024], 1.0, x2f_sb[ti % 3][:, 512:1024],
            ALU.mult, ALU.add, accum_out=st2[:, 1:2]))
        st("vector", "dve")(lambda e, bi=bi: e.scalar_tensor_tensor(
            h2scr[:], h2_sb[bi][:], 1.0, h2_sb[bi][:], ALU.mult, ALU.mult,
            accum_out=st2[:, 2:3]))
        e1 = st("vector", "dve")(lambda e: e.tensor_add(st2[:, 0:1], st2[:, 0:1], st2[:, 1:2]))
        wt("vector", e1)
        e2 = st("vector", "dve")(lambda e: e.tensor_scalar_mul(st2[:, 3:4], st2[:, 0:1], -1.0 / C))
        wt("vector", e2)
        e3 = st("vector", "dve")(lambda e: e.tensor_mul(st2[:, 4:5], st2[:, 3:4], st2[:, 3:4]))
        wt("vector", e3)
        e4 = st("vector", "dve")(lambda e: e.scalar_tensor_tensor(
            st2[:, 5:6], st2[:, 2:3], 1.0 / C, st2[:, 4:5], ALU.mult, ALU.subtract))
        wt("vector", e4)
        ev_var = st("vector", "dve")(lambda e: e.tensor_scalar_add(st2[:, 5:6], st2[:, 5:6], EPS))
        wt("scalar", ev_var)
        st("scalar", "act")(lambda e: e.activation(st2[:, 6:7], st2[:, 5:6], AF.Ln))
        ev_rstd = st("scalar", "act")(lambda e: e.activation(st2[:, 6:7], st2[:, 6:7], AF.Exp, scale=-0.5))
        wt("vector", ev_rstd)
        if ti >= 2:
            wt("vector", ev_ost2[ti - 2])
        ev_o = st("vector", "dve")(lambda e, bi=bi: e.tensor_scalar(
            o_sb[bi][:], h2_sb[bi][:], st2[:, 3:4], st2[:, 6:7], ALU.add, ALU.mult))
        ev_oev[ti] = ev_o
        wt("sync", ev_o)
        st("sync", "qs", 16)(lambda e, ti=ti, bi=bi: e.dma_start(
            out=out_h[128 * ti:128 * (ti + 1), :], in_=o_sb[bi][:]))
        ev_ost2[ti] = cx.ev("qs")
        if ti + 3 < NT:
            wt("sync", ev_oev[ti])
            load_tile2(ti + 3)
    ev_outd = cx.ev("qs")
    wt("sync", ev_outd)
    es_2.close()
    finish(cx)
    return nc


def finish(cx):
    nc = cx.nc
    with nc.Block() as block:
        @block.tensor
        def _(t):
            for f in cx.prog["tensor"]:
                f(t)

        @block.vector
        def _(v):
            for f in cx.prog["vector"]:
                f(v)

        @block.scalar
        def _(s):
            for f in cx.prog["scalar"]:
                f(s)

        @block.gpsimd
        def _(g):
            for f in cx.prog["gpsimd"]:
                f(g)

        @block.sync
        def _(s):
            for f in cx.prog["sync"]:
                f(s)


def _core_inputs(b, ins):
    xb = np.ascontiguousarray(ins["L0_attn_x"][b])
    la = np.asarray(ins["L0_attn_la_coef"], dtype=np.float32)
    kb = np.asarray(ins["L0_attn_kernel_beta"], dtype=np.float32)
    vb = np.asarray(ins["L0_attn_value_beta"], dtype=np.float32)
    vc = np.asarray(ins["L0_attn_v_coef"], dtype=np.float32)
    hog = lambda g: np.concatenate([np.full(64, 2 * g), np.full(64, 2 * g + 1)])
    acol = np.stack([la[hog(g)] for g in range(NG)], axis=1)
    vbcol = np.stack([vb[hog(g)] for g in range(NG)], axis=1)
    bcol = np.broadcast_to(kb[None, :], (128, NH)).copy()
    cbc = np.broadcast_to(np.repeat(vc, 64)[None, :], (128, C)).copy()
    b1 = np.asarray(ins["L0_ffn_b1"][b], dtype=np.float32)
    return dict(
        x=xb,
        xT=np.ascontiguousarray(xb.T),
        Wla=np.ascontiguousarray(ins["L0_attn_W_la"]),
        Wv=np.ascontiguousarray(ins["L0_attn_W_v"]),
        Wp=np.ascontiguousarray(ins["L0_attn_W_proj"]),
        W1=np.ascontiguousarray(ins["L0_ffn_W1"][b]),
        W2T=np.ascontiguousarray(np.asarray(ins["L0_ffn_W2"][b]).T),
        b1rs=np.ascontiguousarray(b1.reshape(32, 128).T),
        b2row=np.ascontiguousarray(np.asarray(ins["L0_ffn_b2"][b], dtype=np.float32)[None, :]),
        acol=np.ascontiguousarray(acol.astype(np.float32)),
        bcol=np.ascontiguousarray(bcol.astype(np.float32)),
        vbcol=np.ascontiguousarray(vbcol.astype(np.float32)),
        cbc=np.ascontiguousarray(cbc.astype(np.float32)),
    )


def kernel(**inputs):
    ins = {k: np.asarray(v) for k, v in inputs.items()}
    nc = build("full")
    in_maps = [_core_inputs(b, ins) for b in range(8)]
    res = run_bass_kernel_spmd(nc, in_maps, core_ids=list(range(8)))
    return np.stack([np.asarray(r["out"], dtype=np.float32) for r in res.results], axis=0)


# version 15
# speedup vs baseline: 1.2174x; 1.2174x over previous
import sys
sys.path.insert(0, "/opt/trn_rl_repo")
import contextlib
import numpy as np
import ml_dtypes
import concourse.bass as bass
import concourse.mybir as mybir
from concourse.bass_utils import run_bass_kernel_spmd

F32 = mybir.dt.float32
BF16 = mybir.dt.bfloat16
AF = mybir.ActivationFunctionType
ALU = mybir.AluOpType
AX = mybir.AxisListType

T = 4096
C = 1024
NH = 16
HS = 64
NT = T // 128
NG = C // 128
QG = T // 512
KC = C // 128
XTP = 4100
EPS = 1e-5

PARANOID = False
ENGS = ("tensor", "vector", "scalar", "gpsimd", "sync")


class Ctx:
    def __init__(self, nc, es):
        self.nc = nc
        self.prog = {e: [] for e in ENGS}
        self.sems = {}
        self.cnt = {}
        for name in ("qs", "qg", "qt", "pe", "dve", "act", "pool",
                     "px0", "px1", "pf0", "pf1", "pst", "pk0", "pk1",
                     "p2a", "p2b", "p2c", "p2d", "p2e", "p2f"):
            self.sems[name] = es.enter_context(nc.semaphore(name))
            self.cnt[name] = 0

    def step(self, eng, mark=None, inc=1):
        ev = None
        if mark is not None:
            self.cnt[mark] += inc
            ev = (mark, self.cnt[mark])

        def deco(f):
            sem = self.sems[mark] if mark is not None else None

            def run(e, _f=f, _sem=sem, _inc=inc):
                inst = _f(e)
                if _sem is not None:
                    inst.then_inc(_sem, _inc)

            self.prog[eng].append(run)
            return ev

        return deco

    def wait(self, eng, *evs):
        for ev in evs:
            if ev is None:
                continue
            name, val = ev
            if val <= 0:
                continue
            sem = self.sems[name]
            self.prog[eng].append(lambda e, _s=sem, _v=val: e.wait_ge(_s, _v))

    def ev(self, mark):
        return (mark, self.cnt[mark])


def build(stage="full"):
    nc = bass.Bass("TRN2", target_bir_lowering=False, debug=False)
    ap = lambda t: t.ap()

    x_h = ap(nc.dram_tensor("x", [T, C], F32, kind="ExternalInput"))
    xT_h = ap(nc.dram_tensor("xT", [C, T], F32, kind="ExternalInput"))
    Wla_h = ap(nc.dram_tensor("Wla", [C, C], F32, kind="ExternalInput"))
    Wv_h = ap(nc.dram_tensor("Wv", [C, C], F32, kind="ExternalInput"))
    Wp_h = ap(nc.dram_tensor("Wp", [C, C], F32, kind="ExternalInput"))
    W1_h = ap(nc.dram_tensor("W1", [4096, C], F32, kind="ExternalInput"))
    W2T_h = ap(nc.dram_tensor("W2T", [4096, C], F32, kind="ExternalInput"))
    b1_h = ap(nc.dram_tensor("b1rs", [128, 32], F32, kind="ExternalInput"))
    b2_h = ap(nc.dram_tensor("b2row", [1, C], F32, kind="ExternalInput"))
    acol_h = ap(nc.dram_tensor("acol", [128, NG], F32, kind="ExternalInput"))
    bcol_h = ap(nc.dram_tensor("bcol", [128, NH], F32, kind="ExternalInput"))
    vbcol_h = ap(nc.dram_tensor("vbcol", [128, NG], F32, kind="ExternalInput"))
    cbc_h = ap(nc.dram_tensor("cbc", [128, C], F32, kind="ExternalInput"))

    out_h = ap(nc.dram_tensor("out", [T, C], F32, kind="ExternalOutput"))
    ktm_h = ap(nc.dram_tensor("ktm_s", [NG, T, 128], BF16,
                              kind="ExternalOutput" if stage in ("k", "kv") else "Internal"))
    v65k = "ExternalOutput" if stage == "kv" else "Internal"
    x1f_kind = "ExternalOutput" if stage == "attn" else "Internal"
    x1T_h = ap(nc.dram_tensor("x1T_s", [C, T], BF16,
                              kind="ExternalOutput" if stage == "attn" else "Internal"))
    x1f_h = ap(nc.dram_tensor("x1f_s", [T, C], F32, kind=x1f_kind))
    if stage == "kv":
        v65d_h = ap(nc.dram_tensor("v65d", [128, NT * NH * 65], BF16, kind="ExternalOutput"))

    tri_strict_np = (np.arange(128)[:, None] < np.arange(128)[None, :]).astype(ml_dtypes.bfloat16)
    tri0_np = tri_strict_np.copy()
    tri0_np[0, 0] = 1
    tri_s_h = nc.inline_tensor(tri_strict_np, "tri_s").ap()
    tri0_h = nc.inline_tensor(tri0_np, "tri0").ap()
    ones64_h = nc.inline_tensor(np.ones((1, 64), ml_dtypes.bfloat16), "ones64c").ap()
    ones128_h = nc.inline_tensor(np.ones((1, 128), ml_dtypes.bfloat16), "ones128c").ap()
    ident_h = nc.inline_tensor(np.eye(128, dtype=ml_dtypes.bfloat16), "identc").ap()

    es = contextlib.ExitStack()
    E = es.enter_context
    SB = lambda n, s, d: E(nc.sbuf_tensor(n, s, d))
    cx = Ctx(nc, es)
    st, wt = cx.step, cx.wait

    psA = E(nc.psum_tensor("psA", [128, 1024], F32))
    psB = E(nc.psum_tensor("psB", [128, 1024], F32))
    psY = [E(nc.psum_tensor(f"psY{i}", [128, 512], F32)) for i in range(2)]
    psR = E(nc.psum_tensor("psR", [128, 512], F32))
    psP = E(nc.psum_tensor("psP", [128, 512], F32))

    acol = SB("acol_sb", [128, NG], F32)
    amc = SB("amc_sb", [128, NG], F32)
    kb2 = SB("kb2_sb", [128, NH], F32)
    nkb2 = SB("nkb2_sb", [128, NH], F32)
    vbc = SB("vbc_sb", [128, NG], F32)
    tri_s = SB("tri_s_sb", [128, 128], BF16)
    tri0 = SB("tri0_sb", [128, 128], BF16)
    ones64 = SB("ones64_sb", [1, 64], BF16)
    ones128 = SB("ones128_sb", [1, 128], BF16)
    ident = SB("ident_sb", [128, 128], BF16)

    for dst, src in [(acol, acol_h), (vbc, vbcol_h), (tri_s, tri_s_h),
                     (tri0, tri0_h), (ones64, ones64_h), (ones128, ones128_h),
                     (ident, ident_h), (kb2, bcol_h)]:
        st("sync", "qs", 16)(lambda e, d=dst, s=src: e.dma_start(out=d[:], in_=s))
    ev_small = cx.ev("qs")
    wt("vector", ev_small)
    st("vector", "dve")(lambda e: e.tensor_scalar(amc[:], acol[:], -1.0, 1.0, ALU.mult, ALU.add))
    st("vector", "dve")(lambda e: e.tensor_scalar(nkb2[:], kb2[:], 10.0, 5.0, ALU.mult, ALU.min))
    ev_coefv = cx.ev("dve")
    wt("scalar", ev_coefv)
    st("scalar", "act")(lambda e: e.activation(kb2[:], nkb2[:], AF.Exp, bias=0.0, scale=2.0))
    st("scalar", "act")(lambda e: e.activation(vbc[:], vbc[:], AF.Exp, bias=0.0, scale=10.0))
    ev_coefs = cx.ev("act")
    wt("vector", ev_coefs)
    st("vector", "dve")(lambda e: e.tensor_scalar_mul(nkb2[:], kb2[:], -1.0))
    ev_nkb2 = cx.ev("dve")

    es_v65 = es.enter_context(contextlib.ExitStack())
    v65 = es_v65.enter_context(nc.sbuf_tensor("v65_sb", [128, NT * NH * 65 + 64], BF16))

    es_xt = es.enter_context(contextlib.ExitStack())
    SBx = lambda n, s, d: es_xt.enter_context(nc.sbuf_tensor(n, s, d))
    xT_bf = SBx("xT_bf", [128, KC * XTP], BF16)

    ev_xtk = {}
    KCPOOL = ("p2a", "p2b", "p2c", "p2d", "p2e", "p2f", "px0", "px1")
    for kc in range(KC):
        st("gpsimd", KCPOOL[kc], 16)(lambda e, kc=kc: e.dma_start(
            out=xT_bf[:, kc * XTP:kc * XTP + T],
            in_=xT_h[128 * kc:128 * (kc + 1), :]))
    for kc in range(KC):
        ev_xtk[kc] = st("gpsimd", KCPOOL[kc], 16)(lambda e, kc=kc: e.dma_start(
            out=Wla_bf[:, kc * C:(kc + 1) * C],
            in_=Wla_h[128 * kc:128 * (kc + 1), :]))
    ev_xt = tuple(ev_xtk[kc] for kc in range(KC))
    wt("vector", *ev_xt)
    st("vector", "dve")(lambda e: e.memset(
        xT_bf[:].rearrange("p (k c) -> p k c", c=XTP)[:, :, T:XTP], 0.0))
    ev_xtpad = cx.ev("dve")

    es_k = es.enter_context(contextlib.ExitStack())
    SBk = lambda n, s, d: es_k.enter_context(nc.sbuf_tensor(n, s, d))
    Wla_bf = SBk("Wla_bf", [128, KC * C], BF16)
    xls = SBk("xls", [128, T], F32)
    krbf = [SBk(f"krbf{i}", [128, T], BF16) for i in range(2)]
    ktm = [SBk(f"ktm{i}", [128, T], BF16) for i in range(2)]
    ksq = SBk("ksq", [128, T], BF16)
    kss = SBk("kss", [128, 64], F32)

    wt("tensor", ev_xtpad)
    ev_scan = {}
    ev_spill = {}
    ev_tposed = {}
    ev_evac = {}
    ps_free = {}
    tp_banks = [psY[0], psY[1], psR, psP]
    ev_tevac = {}
    ev_tpe = {}

    def emit_trip1_and_norm(g):
        bi = g % 2
        wt("tensor", ev_scan[g])
        for bank in range(4):
            if g >= 1:
                wt("tensor", ev_tevac[(g - 1, bank)])
            tpe = None
            for j8 in range(8):
                j = bank * 8 + j8
                tpe = st("tensor", "pe")(lambda e, j=j, j8=j8, bi=bi, bank=bank: e.transpose(
                    tp_banks[bank][:].bitcast(BF16)[:, j8 * 128:(j8 + 1) * 128],
                    krbf[bi][:, 128 * j:128 * (j + 1)], ident[:]))
            wt("vector", tpe)
            if g >= 2:
                wt("vector", ev_spill[g - 2])
            ev_tevac[(g, bank)] = st("vector", "dve")(lambda e, bank=bank, bi=bi: e.tensor_copy(
                ktm[bi][:, bank * 1024:(bank + 1) * 1024], tp_banks[bank][:].bitcast(BF16)))
        ev_tpe[g] = tpe
        wt("scalar", ev_tevac[(g, 3)])
        ev_sq = st("scalar", "act")(lambda e, bi=bi: e.activation(ksq[:], ktm[bi][:], AF.Square))
        wt("vector", ev_sq)
        ev_red = st("vector", "dve")(lambda e: e.reduce_sum(
            kss[:], ksq[:].rearrange("p (j h c) -> p j h c", h=2, c=64), axis=AX.X))
        wt("scalar", ev_red)
        st("scalar", "act")(lambda e: e.activation(kss[:], kss[:], AF.Ln))
        ev_ex = st("scalar", "act")(lambda e: e.activation(kss[:], kss[:], AF.Exp, scale=-0.5))
        wt("vector", ev_ex)
        for col in range(64):
            st("vector", "dve" if col == 63 else None)(lambda e, col=col, bi=bi: e.tensor_scalar(
                ktm[bi][:, col * 64:(col + 1) * 64], ktm[bi][:, col * 64:(col + 1) * 64],
                kss[:, col:col + 1], None, ALU.mult))
        ev_appl = cx.ev("dve")
        wt("sync", ev_appl)
        ev_spill[g] = st("sync", "pk%d" % bi, 16)(lambda e, g=g, bi=bi: e.dma_start(
            out=ktm_h[g].rearrange("(j q) c -> q j c", q=128),
            in_=ktm[bi][:].rearrange("p (j c) -> p j c", c=128)))

    for g in range(NG):
        bi = g % 2
        for tc in range(8):
            slot = (g * 8 + tc) % 4
            pst = psA if slot < 2 else psB
            half = (slot % 2) * 512
            if slot in ps_free:
                wt("tensor", ps_free[slot])
            ev = None
            for kc in range(KC):
                if g == 0 and tc == 0:
                    wt("tensor", ev_xtk[kc])
                ev = st("tensor", "pe" if kc == KC - 1 else None)(
                    lambda e, kc=kc, g=g, tc=tc, pst=pst, half=half: e.matmul(
                        pst[:, half:half + 512],
                        Wla_bf[:, kc * C + g * 128: kc * C + (g + 1) * 128],
                        xT_bf[:, kc * XTP + tc * 512: kc * XTP + (tc + 1) * 512],
                        start=(kc == 0), stop=(kc == KC - 1)))
            wt("vector", ev)
            ps_free[slot] = st("vector", "dve")(
                lambda e, g=g, tc=tc, pst=pst, half=half: e.tensor_scalar(
                    xls[:, tc * 512:(tc + 1) * 512], pst[:, half:half + 512],
                    amc[:, g:g + 1], None, ALU.mult))
        ev_evac[g] = cx.ev("dve")
        if g >= 2:
            wt("vector", ev_tpe[g - 2])
        ev_scan[g] = st("vector", "dve")(lambda e, bi=bi, g=g: e.tensor_tensor_scan(
            krbf[bi][:], acol[:, g:g + 1].broadcast_to([128, T]), xls[:],
            0.0, ALU.mult, ALU.add))
        if g >= 1:
            emit_trip1_and_norm(g - 1)
    emit_trip1_and_norm(NG - 1)

    es_k.close()

    es_v = es.enter_context(contextlib.ExitStack())
    SBv = lambda n, s, d: es_v.enter_context(nc.sbuf_tensor(n, s, d))
    ev_k_done = (cx.ev("dve"), ev_spill[NG - 1])
    Wvc_bf = SBv("Wvc_bf", [128, KC * C], BF16)
    Wv1_bf = SBv("Wv1_bf", [128, KC * C], BF16)
    wtmp = SBv("wtmp", [128, C], F32)
    vsq = [SBv(f"vsq{i}", [128, C], F32) for i in range(2)]
    vss = [SBv(f"vss{i}", [128, NH], F32) for i in range(2)]
    vsb = [SBv(f"vsb{i}", [128, C], F32) for i in range(2)]
    cb = SBv("cb_sb", [128, C], F32)
    cmb = SBv("cmb_sb", [128, C], F32)
    wt("sync", ev_k_done[0])
    st("sync", "qs", 16)(lambda e: e.dma_start(out=cb[:], in_=cbc_h))
    ev_cbl = cx.ev("qs")
    wt("vector", ev_cbl)
    st("vector", "dve")(lambda e: e.tensor_scalar(cmb[:], cb[:], -1.0, 1.0, ALU.mult, ALU.add))

    ev_wvd = {}
    for kc in range(KC):
        wt("sync", ev_k_done[0] if kc == 0 else ev_wvd[kc - 1])
        st("sync", "qs", 16)(lambda e, kc=kc: e.dma_start(out=wtmp[:], in_=Wv_h[128 * kc:128 * (kc + 1), :]))
        ev_l = cx.ev("qs")
        wt("vector", ev_l)
        st("vector", "dve")(lambda e, kc=kc: e.tensor_mul(Wvc_bf[:, kc * C:(kc + 1) * C], wtmp[:], cb[:]))
        ev_wvd[kc] = st("vector", "dve")(lambda e, kc=kc: e.tensor_mul(Wv1_bf[:, kc * C:(kc + 1) * C], wtmp[:], cmb[:]))
    st("vector", "dve")(lambda e: e.memset(v65[:], 1.0))
    ev_v65init = cx.ev("dve")

    ev_vdone = {}
    wt("tensor", ev_wvd[KC - 1])
    for i in range(NT):
        bi = i % 2
        pst = psA if i % 2 == 0 else psB
        if i >= 2:
            wt("tensor", ev_vdone[i - 2])
        ev = None
        first = True
        for half in range(2):
            for kc in range(KC):
                for (w, sh) in ((Wvc_bf, 0), (Wv1_bf, 1)):
                    last = (half == 1 and kc == KC - 1 and sh == 1)
                    ev = st("tensor", "pe" if last else None)(
                        lambda e, kc=kc, w=w, sh=sh, half=half, i=i, pst=pst, first=first,
                               lst=(kc == KC - 1 and sh == 1): e.matmul(
                            pst[:, half * 512:half * 512 + 512],
                            xT_bf[:, kc * XTP + i * 128 + sh: kc * XTP + i * 128 + sh + 128],
                            w[:, kc * C + half * 512: kc * C + half * 512 + 512],
                            start=first, stop=lst))
                    first = False
            first = True
        wt("scalar", ev)
        ev_sq = st("scalar", "act")(lambda e, bi=bi, pst=pst: e.activation(vsq[bi][:], pst[:], AF.Square))
        wt("vector", ev_sq)
        ev_red = st("vector", "dve")(lambda e, bi=bi: e.reduce_sum(
            vss[bi][:], vsq[bi][:].rearrange("p (h c) -> p h c", c=64), axis=AX.X))
        wt("scalar", ev_red)
        st("scalar", "act")(lambda e, bi=bi: e.activation(vss[bi][:], vss[bi][:], AF.Ln))
        ev_ex = st("scalar", "act")(lambda e, bi=bi: e.activation(vss[bi][:], vss[bi][:], AF.Exp, scale=-0.5))
        wt("vector", ev_ex)
        st("vector", "dve")(lambda e, bi=bi: e.tensor_copy(
            vsb[bi][:].rearrange("p (h c) -> p h c", c=64),
            vss[bi][:].rearrange("p (h c) -> p h c", c=1).broadcast_to([128, NH, 64])))
        if i == 0:
            wt("vector", ev_v65init)
        ev_vdone[i] = st("vector", "dve")(lambda e, bi=bi, i=i, pst=pst: e.tensor_mul(
            v65[:, i * NH * 65:(i + 1) * NH * 65].rearrange("p (h c) -> p h c", c=65)[:, :, 0:64],
            pst[:].rearrange("p (h c) -> p h c", c=64),
            vsb[bi][:].rearrange("p (h c) -> p h c", c=64)))

    if stage == "kv":
        wt("sync", ev_vdone[NT - 1])
        st("sync", "qs", 16)(lambda e: e.dma_start(out=v65d_h, in_=v65[:]))
        ev_fin = cx.ev("qs")
        wt("sync", ev_fin)
        es_v.close(); es_xt.close()
        finish(cx)
        return nc
    es_v.close(); es_xt.close()
    if stage == "k":
        wt("sync", ev_spill[NG - 1])
        finish(cx)
        return nc

    es_c = es.enter_context(contextlib.ExitStack())
    SBc = lambda n, s, d: es_c.enter_context(nc.sbuf_tensor(n, s, d))
    kT = SBc("kT_sb", [128, NG * T], BF16)
    Wp_bf = SBc("Wp_bf", [128, KC * C], BF16)
    es_w = contextlib.ExitStack()
    wtmp2 = es_w.enter_context(nc.sbuf_tensor("wtmp2", [128, C], F32))
    P_sb = None

    ev_1a_done = (cx.ev("dve"), cx.ev("pe"), cx.ev("act"))

    ev_kld = {}
    wt("sync", ev_1a_done[0], ev_1a_done[1])

    def reload_wave(gq):
        for kj in range(4 * gq, 4 * gq + 4):
            for g in range(NG):
                st("sync", "qt", 16)(lambda e, g=g, kj=kj: e.dma_start_transpose(
                    kT[:, g * T + 128 * kj: g * T + 128 * (kj + 1)],
                    ktm_h[g, 128 * kj:128 * (kj + 1), :]))
        ev_kld[gq] = (cx.ev("qt"),)

    wt("sync", ev_spill[NG - 1], ev_spill[NG - 2])
    reload_wave(0)
    reload_wave(1)

    ev_wpd = {}
    for gc in range(KC):
        if gc > 0:
            wt("sync", ev_wpd[gc - 1])
        st("sync", "qs", 16)(lambda e, gc=gc: e.dma_start(out=wtmp2[:], in_=Wp_h[128 * gc:128 * (gc + 1), :]))
        ev_l = cx.ev("qs")
        wt("vector", ev_l)
        ev_wpd[gc] = st("vector", "dve")(lambda e, gc=gc: e.tensor_scalar(
            Wp_bf[:, gc * C:(gc + 1) * C], wtmp2[:], vbc[:, gc:gc + 1], None, ALU.mult))
    es_w.close()
    P_sb = [SBc(f"P{i}", [128, 1024], BF16) for i in range(2)]
    ynT = [SBc(f"ynT{i}", [128, NG * 512], BF16) for i in range(2)]
    rrow = SBc("rrow", [1, 512], F32)
    rhi = [SBc(f"rhi{i}", [1, 512], BF16) for i in range(2)]
    rlo = [SBc(f"rlo{i}", [1, 512], BF16) for i in range(2)]
    rbc = SBc("rbc", [64, 512], F32)
    yst = [SBc(f"yst{i}", [64, 512], BF16) for i in range(2)]
    lrow = SBc("lrow2", [1, 1024], F32)
    x_sb = [SBc(f"x_sb{i}", [128, C], F32) for i in range(2)]
    h_sb = [SBc(f"h_sb{i}", [128, C], F32) for i in range(2)]
    hscr = SBc("hscr", [128, C], BF16)
    stat = SBc("stat", [128, 8], F32)
    x1f_sb = SBc("x1f", [128, C], F32)
    x1b_sb = SBc("x1b", [128, C], BF16)
    x1stage = SBc("x1stage", [128, C], BF16)

    x_ev = {}

    def load_x(ti):
        wt("sync", ev_1a_done[0])
        return st("sync", "px%d" % (ti % 2), 16)(lambda e, ti=ti: e.dma_start(
            out=x_sb[ti % 2][:], in_=x_h[128 * ti:128 * (ti + 1), :]))

    x_ev[0] = load_x(0)
    x_ev[1] = load_x(1)

    gS = 0
    pend_lpath = []
    pend_lsc = []
    pend_pv = []
    pend_proj = []
    rr_par = lambda hi_: hi_ % 2
    rr_t = lambda hb: (rrow, rhi[hb], rlo[hb])

    def emit_pv(key, kj, pbuf, h_, hi_, ypar_, nkj_, gq_, slot, off, mev):
        if kj == 0 and hi_ >= 2:
            wt("tensor", ev_yev[hi_ - 2])
        wt("tensor", ev_expg[key])
        if mev is not None:
            wt("tensor", mev)
        pvev = st("tensor", "pe")(
            lambda e, kj=kj, h=h_, off=off, pbuf=pbuf, ypar=ypar_, slot=slot,
                   fi=(kj == 0), la=(kj == nkj_ - 1): e.matmul(
                psY[ypar][:, off:512],
                v65[:, kj * NH * 65 + h * 65: kj * NH * 65 + h * 65 + 128],
                pbuf[:, slot * 512 + off: (slot + 1) * 512],
                start=fi, stop=la))
        ev_pvg[key] = pvev
        return pvev

    def emit_lsc(hi_, g_, e64_, par_, ypar_, nkj_):
        hb = rr_par(hi_)
        wt("scalar", ev_yev[hi_])
        if hi_ >= 1:
            wt("scalar", ev_rlo_d[hi_ - 1])
        st("scalar", "act")(lambda e, hb=hb: e.activation(
            rr_t(hb)[0][:], lrow[0:1, hb * 512:hb * 512 + 512], AF.Ln))
        ev_r = st("scalar", "act")(lambda e, hb=hb: e.activation(
            rr_t(hb)[0][:], rr_t(hb)[0][:], AF.Exp, scale=-1.0))
        ev_expr[hi_] = ev_r
        wt("vector", ev_r)
        wt("vector", ev_lbcd.get(hi_ - 2))
        st("vector", "dve")(lambda e, hb=hb: e.tensor_copy(rr_t(hb)[1][:], rr_t(hb)[0][:]))
        ev_rlo_d[hi_] = st("vector", "dve")(lambda e, hb=hb: e.tensor_sub(
            rr_t(hb)[2][:], rr_t(hb)[0][:], rr_t(hb)[1][:]))
        pend_lpath.append((hi_, g_, e64_, par_, ypar_))

    def emit_lpath_te(hi_, g_, e64_, par_, ypar_):
        hb = rr_par(hi_)
        wt("tensor", ev_rlo_d[hi_])
        if hi_ >= 1:
            wt("tensor", ev_ynorm[hi_ - 1])
        if (hi_ % NH) == 0 and hi_ >= NH:
            wt("tensor", ev_lnv.get(4 * (hi_ // NH) - 1))
        st("tensor", None)(lambda e, hb=hb: e.matmul(psR[0:64, :], ones64[:], rr_t(hb)[1][:], start=True, stop=False))
        ev_lbc = st("tensor", "pe")(lambda e, hb=hb: e.matmul(psR[0:64, :], ones64[:], rr_t(hb)[2][:], start=False, stop=True))
        ev_lbcd[hi_] = ev_lbc
        wt("vector", ev_lbc)
        st("vector", "dve")(lambda e: e.tensor_copy(rbc[:], psR[0:64, :]))
        if (hi_ % NH) == 0 and hi_ // NH >= 2:
            wt("vector", ev_projmm[4 * (hi_ // NH - 2) + 3])
        ev_ynorm[hi_] = st("vector", "dve")(lambda e, g=g_, e64=e64_, par=par_, hb=rr_par(hi_): e.tensor_mul(
            ynT[par][e64:e64 + 64, g * 512:(g + 1) * 512], yst[hb][:], rbc[:]))

    def emit_projgq(gq, par, ev_yn_all):
        for qt in range(4):
            ti = 4 * gq + qt
            bi = ti % 2
            wt("tensor", ev_yn_all)
            if ti >= 1:
                wt("tensor", ev_lnv.get(ti - 1))
            pe_ev = None
            for half, pp in ((0, psP), (1, psR)):
                for gc in range(KC):
                    pe_ev = st("tensor", "pe" if gc == KC - 1 else None)(
                        lambda e, gc=gc, half=half, qt=qt, par=par, pp=pp: e.matmul(
                            pp[:, 0:512],
                            ynT[par][:, gc * 512 + qt * 128: gc * 512 + (qt + 1) * 128],
                            Wp_bf[:, gc * C + half * 512: gc * C + half * 512 + 512],
                            start=(gc == 0), stop=(gc == KC - 1)))
            ev_projmm[ti] = pe_ev
            wt("vector", pe_ev, x_ev[ti])
            st("vector", "dve")(lambda e, bi=bi: e.scalar_tensor_tensor(
                h_sb[bi][:, 0:512], psP[:, 0:512], 1.0, x_sb[bi][:, 0:512],
                ALU.mult, ALU.add, accum_out=stat[:, 0:1]))
            st("vector", "dve")(lambda e, bi=bi: e.scalar_tensor_tensor(
                h_sb[bi][:, 512:1024], psR[:, 0:512], 1.0, x_sb[bi][:, 512:1024],
                ALU.mult, ALU.add, accum_out=stat[:, 1:2]))
            st("vector", "dve")(lambda e, bi=bi: e.scalar_tensor_tensor(
                hscr[:], h_sb[bi][:], 1.0, h_sb[bi][:], ALU.mult, ALU.mult,
                accum_out=stat[:, 2:3]))
            e1 = st("vector", "dve")(lambda e: e.tensor_add(stat[:, 0:1], stat[:, 0:1], stat[:, 1:2]))
            wt("vector", e1)
            e2 = st("vector", "dve")(lambda e: e.tensor_scalar_mul(stat[:, 3:4], stat[:, 0:1], -1.0 / C))
            wt("vector", e2)
            e3 = st("vector", "dve")(lambda e: e.tensor_mul(stat[:, 4:5], stat[:, 3:4], stat[:, 3:4]))
            wt("vector", e3)
            e4 = st("vector", "dve")(lambda e: e.scalar_tensor_tensor(
                stat[:, 5:6], stat[:, 2:3], 1.0 / C, stat[:, 4:5], ALU.mult, ALU.subtract))
            wt("vector", e4)
            ev_var = st("vector", "dve")(lambda e: e.tensor_scalar_add(stat[:, 5:6], stat[:, 5:6], EPS))
            wt("scalar", ev_var)
            st("scalar", "act")(lambda e: e.activation(stat[:, 6:7], stat[:, 5:6], AF.Ln))
            ev_rstd = st("scalar", "act")(lambda e: e.activation(stat[:, 6:7], stat[:, 6:7], AF.Exp, scale=-0.5))
            wt("vector", ev_rstd)
            if ti >= 1:
                wt("vector", ev_spl_f[ti - 1])
            ev_x1 = st("vector", "dve")(lambda e, bi=bi: e.tensor_scalar(
                x1f_sb[:], h_sb[bi][:], stat[:, 3:4], stat[:, 6:7], ALU.add, ALU.mult))
            ev_lnv[ti] = ev_x1
            if ti >= 1:
                wt("vector", ev_t_spl[ti - 1])
            ev_x1b = st("vector", "dve")(lambda e: e.tensor_copy(x1b_sb[:], x1f_sb[:]))
            wt("sync", ev_x1b)
            ev_spl_f[ti] = st("sync", "pf%d" % (ti % 2), 16)(lambda e, ti=ti: e.dma_start(
                out=x1f_h[128 * ti:128 * (ti + 1), :], in_=x1f_sb[:]))
            if ti >= 1:
                wt("sync", ev_st_spl[ti - 1])
            for gc in range(NG):
                st("sync", "qt", 16)(lambda e, gc=gc: e.dma_start_transpose(
                    x1stage[:, gc * 128:(gc + 1) * 128], x1b_sb[:, gc * 128:(gc + 1) * 128]))
            ev_t_spl[ti] = cx.ev("qt")
            wt("sync", ev_t_spl[ti])
            ev_st_spl[ti] = st("sync", "pst", 16)(lambda e, ti=ti: e.dma_start(
                out=x1T_h.rearrange("(gc p) t -> p gc t", p=128)[:, :, 128 * ti:128 * (ti + 1)],
                in_=x1stage[:].rearrange("p (gc c) -> p gc c", c=128)))
            if ti + 2 < NT:
                wt("sync", ev_x1)
                x_ev[ti + 2] = st("sync", "px%d" % (ti % 2), 16)(lambda e, ti=ti: e.dma_start(
                    out=x_sb[ti % 2][:], in_=x_h[128 * (ti + 2):128 * (ti + 3), :]))

    ev_expg = {}
    ev_smm = {}
    ev_pvg = {}
    ev_ynorm = {}
    ev_projmm = {}
    ev_lnv = {}
    ev_rlo_d = {}
    ev_yev = {}
    ev_expr = {}
    ev_lbcd = {}
    ev_spl_f = {}
    ev_t_spl = {}
    ev_st_spl = {}
    ev_ost = {}
    hi = 0
    for gq in range(QG):
        nkj = 4 * gq + 4
        par = gq % 2
        if gq + 2 < QG:
            reload_wave(gq + 2)
        for hp in range(NH // 2):
            g = hp
            nb = (nkj + 1) // 2
            if hp == 1 and pend_proj:
                emit_projgq(*pend_proj.pop(0))
            if hi == 0:
                wt("scalar", ev_nkb2)
            for kj in range(nkj):
                slot = kj % 2
                off = 128 * max(0, kj - 4 * gq)
                for side, e64, pst, pbuf in ((0, 0, psA, P_sb[0]), (1, 64, psB, P_sb[1])):
                    hix = hi + side
                    if kj >= 2:
                        wt("tensor", ev_expg[(hix, kj - 2)])
                    if hi == 0 and kj == 0 and side == 0:
                        wt("tensor", ev_kld[gq][0], ev_wpd[KC - 1])
                    elif kj == 0 and hp == 0 and side == 0:
                        wt("tensor", ev_kld[gq][0])
                    ev_smm[(hix, kj)] = st("tensor", "pe")(
                        lambda e, g=g, e64=e64, kj=kj, gq=gq, off=off, pst=pst, slot=slot: e.matmul(
                            pst[:, slot * 512 + off: (slot + 1) * 512],
                            kT[e64:e64 + 64, g * T + 128 * kj: g * T + 128 * (kj + 1)],
                            kT[e64:e64 + 64, g * T + 512 * gq + off: g * T + 512 * (gq + 1)],
                            start=True, stop=True, tile_position=(e64, 0)))
                if len(pend_pv) >= 2:
                    emit_pv(*pend_pv.pop(0))
                    emit_pv(*pend_pv.pop(0))
                for side, e64, pst, pbuf in ((0, 0, psA, P_sb[0]), (1, 64, psB, P_sb[1])):
                    h = 2 * hp + side
                    hix = hi + side
                    wt("scalar", ev_smm[(hix, kj)])
                    if kj >= 2:
                        wt("scalar", ev_pvg[(hix, kj - 2)])
                    ev_expg[(hix, kj)] = st("scalar", "act")(
                        lambda e, pst=pst, pbuf=pbuf, h=h, slot=slot, off=off: e.activation(
                            pbuf[:, slot * 512 + off: (slot + 1) * 512],
                            pst[:, slot * 512 + off: (slot + 1) * 512], AF.Exp,
                            bias=nkb2[:, h:h + 1], scale=kb2[:, h:h + 1]))
                    mev = None
                    if kj >= 4 * gq:
                        tri = tri0 if (gq == 0 and kj == 0) else tri_s
                        wt("gpsimd", ev_expg[(hix, kj)])
                        mev = st("gpsimd", "pool")(lambda e, pbuf=pbuf, slot=slot, off=off, tri=tri: e.tensor_mul(
                            pbuf[:, slot * 512 + off: slot * 512 + off + 128],
                            pbuf[:, slot * 512 + off: slot * 512 + off + 128], tri[:]))
                    pend_pv.append(((hix, kj), kj, pbuf, h, hix, side, nkj, gq, slot, off, mev))
                    if side == 1 and kj in (1, 2) and pend_lsc:
                        emit_lsc(*pend_lsc.pop(0))
                    if side == 1 and kj in (3, 4) and pend_lpath:
                        emit_lpath_te(*pend_lpath.pop(0))
            while pend_pv:
                pvev = emit_pv(*pend_pv.pop(0))
            while pend_lpath:
                emit_lpath_te(*pend_lpath.pop(0))
            for side in range(2):
                hix = hi + side
                hb = rr_par(hix)
                wt("vector", ev_pvg[(hix, nkj - 1)])
                if hix >= 2:
                    wt("vector", ev_expr.get(hix - 2))
                st("vector", "dve")(lambda e, ypar=side, hb=hb: e.tensor_copy(
                    yst[hb][:], psY[ypar][0:64, :]))
                ev_yev[hix] = st("vector", "dve")(lambda e, ypar=side, hb=hb: e.tensor_copy(
                    lrow[0:1, hb * 512:hb * 512 + 512], psY[ypar][64:65, :]))
                pend_lsc.append((hix, g, side * 64, par, side, nkj))
            hi += 2
        while pend_lsc:
            emit_lsc(*pend_lsc.pop(0))
        while pend_lpath:
            emit_lpath_te(*pend_lpath.pop(0))
        if gq == 0:
            wt("vector", ev_ynorm[hi - 1])
            for gc in range(NG):
                st("vector", "dve")(lambda e, gc=gc: e.memset(ynT[0][:, gc * 512:gc * 512 + 1], 0.0))
        ev_yn_all = cx.ev("dve")
        pend_proj.append((gq, par, ev_yn_all))

    while pend_proj:
        emit_projgq(*pend_proj.pop(0))

    ev_1c_spill_done = cx.ev("qs")
    ev_1c_pools = (cx.ev("pf0"), cx.ev("pf1"), cx.ev("pst"), cx.ev("px0"), cx.ev("px1"))
    if stage == "attn":
        wt("sync", ev_1c_spill_done)
        es_c.close()
        finish(cx)
        return nc
    es_c.close()
    es_v65.close()

    es_2 = es.enter_context(contextlib.ExitStack())
    SB2 = lambda n, s, d: es_2.enter_context(nc.sbuf_tensor(n, s, d))
    W2T_bf = SB2("W2T_bf", [128, 32 * C], BF16)
    w1s = [SB2(f"w1s{i}", [128, 32 * 128], BF16) for i in range(2)]
    FF_bf = SB2("FF_bf", [128, KC * C], BF16)
    bias2 = SB2("bias2", [1, C], BF16)
    b1_bf = SB2("b1_bf", [128, 32], BF16)
    x1t_sb = [SB2(f"x1t{i}", [128, NG * 128], BF16) for i in range(3)]
    b2row = SB2("b2row_sb", [1, C], F32)
    x2f_sb = [SB2(f"x2f{i}", [128, C], F32) for i in range(3)]
    h2_sb = [SB2(f"h2_sb{i}", [128, C], F32) for i in range(2)]
    h2scr = SB2("h2scr", [128, C], BF16)
    st2 = SB2("st2", [128, 8], F32)
    o_sb = [SB2(f"o_sb{i}", [128, C], F32) for i in range(2)]

    ev_1c_done = (cx.ev("pe"), cx.ev("dve"), cx.ev("act"), ev_1c_spill_done)
    wt("sync", *ev_1c_done[:3])
    st("sync", "qs", 16)(lambda e: e.dma_start(out=b2row[:], in_=b2_h))
    ev_b2l = cx.ev("qs")
    wt("gpsimd", *ev_1c_done[:3])
    wstg = [SB2(f"wstg{i}", [128, 4 * C], F32) for i in range(2)]
    ev_w2c = {}
    for q4 in range(8):
        if q4 >= 2:
            wt("sync", ev_w2c[q4 - 2])
        st("sync", ("pf0", "pf1")[q4 % 2], 16)(lambda e, q4=q4: e.dma_start(
            out=wstg[q4 % 2][:].rearrange("p (oc c) -> p oc c", c=C),
            in_=W2T_h[q4 * 512:(q4 + 1) * 512, :].rearrange("(oc p) c -> p oc c", p=128)))
        evl = cx.ev(("pf0", "pf1")[q4 % 2])
        wt("vector", evl)
        ev_w2c[q4] = st("vector", "dve")(lambda e, q4=q4: e.tensor_copy(
            W2T_bf[:, q4 * 4 * C:(q4 + 1) * 4 * C], wstg[q4 % 2][:]))
    ev_w2 = ev_w2c[7]
    st("gpsimd", "qg", 16)(lambda e: e.dma_start(out=b1_bf[:], in_=b1_h))
    ev_b1 = cx.ev("qg")
    wt("tensor", ev_w2, ev_b1)
    for half, pb in ((0, psY[0]), (1, psY[1])):
        for oc in range(32):
            st("tensor", "pe" if oc == 31 else None)(lambda e, oc=oc, half=half, pb=pb: e.matmul(
                pb[0:1, :], b1_bf[:, oc:oc + 1],
                W2T_bf[:, oc * C + half * 512: oc * C + half * 512 + 512],
                start=(oc == 0), stop=(oc == 31)))
    ev_b2mm = cx.ev("pe")
    wt("vector", ev_b2mm, ev_b2l)
    st("vector", "dve")(lambda e: e.tensor_add(bias2[0:1, 0:512], psY[0][0:1, :], b2row[0:1, 0:512]))
    ev_bias2 = st("vector", "dve")(lambda e: e.tensor_add(bias2[0:1, 512:1024], psY[1][0:1, :], b2row[0:1, 512:1024]))

    ev_ffd = {}
    ev_w1l = {}
    ev_ffmm = {}

    w1f = SB2("w1f", [128, 32 * 128], F32)

    def load_w1(hc):
        if hc >= 1:
            wt("sync", ev_w1l[hc - 1])
        for h2_ in range(2):
            st("sync", "qs", 16)(lambda e, hc=hc, h2_=h2_: e.dma_start(
                out=w1f[:, h2_ * 16 * 128:(h2_ + 1) * 16 * 128].rearrange("p (oc c) -> p oc c", c=128),
                in_=W1_h[h2_ * 2048:(h2_ + 1) * 2048, 128 * hc:128 * (hc + 1)].rearrange("(oc p) c -> p oc c", p=128)))
        evl = cx.ev("qs")
        wt("vector", evl)
        if hc >= 2:
            wt("vector", ev_ffmm[hc - 2])
        ev_w1l[hc] = st("vector", "dve")(lambda e, hc=hc: e.tensor_copy(w1s[hc % 2][:], w1f[:]))

    load_w1(0)
    load_w1(1)
    for hc in range(KC):
        wt("tensor", ev_w1l[hc])
        if hc >= 2:
            wt("tensor", ev_ffd[hc - 2])
        pfold = psA if hc % 2 == 0 else psB
        ev = None
        for half in range(2):
            for oc in range(32):
                ev = st("tensor", "pe" if (oc == 31 and half == 1) else None)(
                    lambda e, oc=oc, hc=hc, half=half, pfold=pfold: e.matmul(
                        pfold[:, half * 512:half * 512 + 512],
                        w1s[hc % 2][:, oc * 128:(oc + 1) * 128],
                        W2T_bf[:, oc * C + half * 512: oc * C + half * 512 + 512],
                        start=(oc == 0), stop=(oc == 31)))
        ev_ffmm[hc] = ev
        wt("vector", ev)
        ev_ffd[hc] = st("vector", "dve")(lambda e, hc=hc, pfold=pfold: e.tensor_copy(FF_bf[:, hc * C:(hc + 1) * C], pfold[:]))
        if hc + 2 < KC:
            load_w1(hc + 2)

    if stage == "ffa":
        ffd_h = ap(nc.dram_tensor("ffd", [128, KC * C], BF16, kind="ExternalOutput"))
        b2d_h = ap(nc.dram_tensor("b2d", [1, C], BF16, kind="ExternalOutput"))
        wt("sync", ev_ffd[KC - 1], ev_bias2)
        st("sync", "qs", 16)(lambda e: e.dma_start(out=ffd_h, in_=FF_bf[:]))
        st("sync", "qs", 16)(lambda e: e.dma_start(out=b2d_h, in_=bias2[:]))
        evf = cx.ev("qs")
        wt("sync", evf)
        es_2.close()
        finish(cx)
        return nc

    ev_x1t = {}
    ev_x1f = {}
    ev_oev = {}
    ev_ost2 = {}
    wt("sync", ev_1c_done[3], *ev_1c_pools)

    def load_tile2(ti):
        e1 = st("sync", ("p2a", "p2b", "p2c")[ti % 3], 16)(lambda e, ti=ti: e.dma_start(
            out=x1t_sb[ti % 3][:].rearrange("p (gc c) -> p gc c", c=128),
            in_=x1T_h.rearrange("(gc p) t -> p gc t", p=128)[:, :, 128 * ti:128 * (ti + 1)]))
        ev_x1t[ti] = (e1,)
        ev_x1f[ti] = st("sync", ("p2d", "p2e", "p2f")[ti % 3], 16)(lambda e, ti=ti: e.dma_start(
            out=x2f_sb[ti % 3][:], in_=x1f_h[128 * ti:128 * (ti + 1), :]))

    load_tile2(0)
    load_tile2(1)
    load_tile2(2)
    for ti in range(NT):
        bi = ti % 2
        pst2 = psB if ti % 2 == 0 else psA
        wt("tensor", ev_x1t[ti][0])
        if ti == 0:
            wt("tensor", ev_ffd[KC - 1])
        if ti >= 2:
            wt("tensor", ev_oev[ti - 2])
        if PARANOID and ti >= 1:
            wt("tensor", ev_oev[ti - 1])
            wt("tensor", ev_ost2.get(ti - 1))
        pe_ev = None
        for half in range(2):
            for gc in range(KC):
                pe_ev = st("tensor", None)(lambda e, gc=gc, half=half, ti=ti, bi=bi, pst2=pst2: e.matmul(
                    pst2[:, half * 512:half * 512 + 512],
                    x1t_sb[ti % 3][:, gc * 128:(gc + 1) * 128],
                    FF_bf[:, gc * C + half * 512: gc * C + half * 512 + 512],
                    start=(gc == 0), stop=False))
            pe_ev = st("tensor", "pe" if half == 1 else None)(
                lambda e, half=half, pst2=pst2: e.matmul(
                    pst2[:, half * 512:half * 512 + 512], ones128[:],
                    bias2[0:1, half * 512:half * 512 + 512], start=False, stop=True))
        if ti == 0:
            wt("vector", ev_bias2)
        wt("vector", pe_ev, ev_x1f[ti])
        st("vector", "dve")(lambda e, bi=bi, ti=ti, pst2=pst2: e.scalar_tensor_tensor(
            h2_sb[bi][:, 0:512], pst2[:, 0:512], 1.0, x2f_sb[ti % 3][:, 0:512],
            ALU.mult, ALU.add, accum_out=st2[:, 0:1]))
        st("vector", "dve")(lambda e, bi=bi, ti=ti, pst2=pst2: e.scalar_tensor_tensor(
            h2_sb[bi][:, 512:1024], pst2[:, 512:1024], 1.0, x2f_sb[ti % 3][:, 512:1024],
            ALU.mult, ALU.add, accum_out=st2[:, 1:2]))
        st("vector", "dve")(lambda e, bi=bi: e.scalar_tensor_tensor(
            h2scr[:], h2_sb[bi][:], 1.0, h2_sb[bi][:], ALU.mult, ALU.mult,
            accum_out=st2[:, 2:3]))
        e1 = st("vector", "dve")(lambda e: e.tensor_add(st2[:, 0:1], st2[:, 0:1], st2[:, 1:2]))
        wt("vector", e1)
        e2 = st("vector", "dve")(lambda e: e.tensor_scalar_mul(st2[:, 3:4], st2[:, 0:1], -1.0 / C))
        wt("vector", e2)
        e3 = st("vector", "dve")(lambda e: e.tensor_mul(st2[:, 4:5], st2[:, 3:4], st2[:, 3:4]))
        wt("vector", e3)
        e4 = st("vector", "dve")(lambda e: e.scalar_tensor_tensor(
            st2[:, 5:6], st2[:, 2:3], 1.0 / C, st2[:, 4:5], ALU.mult, ALU.subtract))
        wt("vector", e4)
        ev_var = st("vector", "dve")(lambda e: e.tensor_scalar_add(st2[:, 5:6], st2[:, 5:6], EPS))
        wt("scalar", ev_var)
        st("scalar", "act")(lambda e: e.activation(st2[:, 6:7], st2[:, 5:6], AF.Ln))
        ev_rstd = st("scalar", "act")(lambda e: e.activation(st2[:, 6:7], st2[:, 6:7], AF.Exp, scale=-0.5))
        wt("vector", ev_rstd)
        if ti >= 2:
            wt("vector", ev_ost2[ti - 2])
        ev_o = st("vector", "dve")(lambda e, bi=bi: e.tensor_scalar(
            o_sb[bi][:], h2_sb[bi][:], st2[:, 3:4], st2[:, 6:7], ALU.add, ALU.mult))
        ev_oev[ti] = ev_o
        wt("sync", ev_o)
        st("sync", "qs", 16)(lambda e, ti=ti, bi=bi: e.dma_start(
            out=out_h[128 * ti:128 * (ti + 1), :], in_=o_sb[bi][:]))
        ev_ost2[ti] = cx.ev("qs")
        if ti + 3 < NT:
            wt("sync", ev_oev[ti])
            load_tile2(ti + 3)
    ev_outd = cx.ev("qs")
    wt("sync", ev_outd)
    es_2.close()
    finish(cx)
    return nc


def finish(cx):
    nc = cx.nc
    with nc.Block() as block:
        @block.tensor
        def _(t):
            for f in cx.prog["tensor"]:
                f(t)

        @block.vector
        def _(v):
            for f in cx.prog["vector"]:
                f(v)

        @block.scalar
        def _(s):
            for f in cx.prog["scalar"]:
                f(s)

        @block.gpsimd
        def _(g):
            for f in cx.prog["gpsimd"]:
                f(g)

        @block.sync
        def _(s):
            for f in cx.prog["sync"]:
                f(s)


def _core_inputs(b, ins):
    xb = np.ascontiguousarray(ins["L0_attn_x"][b])
    la = np.asarray(ins["L0_attn_la_coef"], dtype=np.float32)
    kb = np.asarray(ins["L0_attn_kernel_beta"], dtype=np.float32)
    vb = np.asarray(ins["L0_attn_value_beta"], dtype=np.float32)
    vc = np.asarray(ins["L0_attn_v_coef"], dtype=np.float32)
    hog = lambda g: np.concatenate([np.full(64, 2 * g), np.full(64, 2 * g + 1)])
    acol = np.stack([la[hog(g)] for g in range(NG)], axis=1)
    vbcol = np.stack([vb[hog(g)] for g in range(NG)], axis=1)
    bcol = np.broadcast_to(kb[None, :], (128, NH)).copy()
    cbc = np.broadcast_to(np.repeat(vc, 64)[None, :], (128, C)).copy()
    b1 = np.asarray(ins["L0_ffn_b1"][b], dtype=np.float32)
    return dict(
        x=xb,
        xT=np.ascontiguousarray(xb.T),
        Wla=np.ascontiguousarray(ins["L0_attn_W_la"]),
        Wv=np.ascontiguousarray(ins["L0_attn_W_v"]),
        Wp=np.ascontiguousarray(ins["L0_attn_W_proj"]),
        W1=np.ascontiguousarray(ins["L0_ffn_W1"][b]),
        W2T=np.ascontiguousarray(np.asarray(ins["L0_ffn_W2"][b]).T),
        b1rs=np.ascontiguousarray(b1.reshape(32, 128).T),
        b2row=np.ascontiguousarray(np.asarray(ins["L0_ffn_b2"][b], dtype=np.float32)[None, :]),
        acol=np.ascontiguousarray(acol.astype(np.float32)),
        bcol=np.ascontiguousarray(bcol.astype(np.float32)),
        vbcol=np.ascontiguousarray(vbcol.astype(np.float32)),
        cbc=np.ascontiguousarray(cbc.astype(np.float32)),
    )


def kernel(**inputs):
    ins = {k: np.asarray(v) for k, v in inputs.items()}
    nc = build("full")
    in_maps = [_core_inputs(b, ins) for b in range(8)]
    res = run_bass_kernel_spmd(nc, in_maps, core_ids=list(range(8)))
    return np.stack([np.asarray(r["out"], dtype=np.float32) for r in res.results], axis=0)


# version 16 (final)
# speedup vs baseline: 1.2344x; 1.0140x over previous
import sys
sys.path.insert(0, "/opt/trn_rl_repo")
import contextlib
import numpy as np
import ml_dtypes
import concourse.bass as bass
import concourse.mybir as mybir
from concourse.bass_utils import run_bass_kernel_spmd

F32 = mybir.dt.float32
BF16 = mybir.dt.bfloat16
AF = mybir.ActivationFunctionType
ALU = mybir.AluOpType
AX = mybir.AxisListType

T = 4096
C = 1024
NH = 16
HS = 64
NT = T // 128
NG = C // 128
QG = T // 512
KC = C // 128
XTP = 4100
EPS = 1e-5

PARANOID = False
ENGS = ("tensor", "vector", "scalar", "gpsimd", "sync")


class Ctx:
    def __init__(self, nc, es):
        self.nc = nc
        self.prog = {e: [] for e in ENGS}
        self.sems = {}
        self.cnt = {}
        for name in ("qs", "qg", "qt", "pe", "dve", "act", "pool",
                     "px0", "px1", "pf0", "pf1", "pst", "pk0", "pk1",
                     "p2a", "p2b", "p2c", "p2d", "p2e", "p2f"):
            self.sems[name] = es.enter_context(nc.semaphore(name))
            self.cnt[name] = 0

    def step(self, eng, mark=None, inc=1):
        ev = None
        if mark is not None:
            self.cnt[mark] += inc
            ev = (mark, self.cnt[mark])

        def deco(f):
            sem = self.sems[mark] if mark is not None else None

            def run(e, _f=f, _sem=sem, _inc=inc):
                inst = _f(e)
                if _sem is not None:
                    inst.then_inc(_sem, _inc)

            self.prog[eng].append(run)
            return ev

        return deco

    def wait(self, eng, *evs):
        for ev in evs:
            if ev is None:
                continue
            name, val = ev
            if val <= 0:
                continue
            sem = self.sems[name]
            self.prog[eng].append(lambda e, _s=sem, _v=val: e.wait_ge(_s, _v))

    def ev(self, mark):
        return (mark, self.cnt[mark])


def build(stage="full"):
    nc = bass.Bass("TRN2", target_bir_lowering=False, debug=False)
    ap = lambda t: t.ap()

    x_h = ap(nc.dram_tensor("x", [T, C], F32, kind="ExternalInput"))
    xT_h = ap(nc.dram_tensor("xT", [C, T], F32, kind="ExternalInput"))
    Wla_h = ap(nc.dram_tensor("Wla", [C, C], F32, kind="ExternalInput"))
    Wv_h = ap(nc.dram_tensor("Wv", [C, C], F32, kind="ExternalInput"))
    Wp_h = ap(nc.dram_tensor("Wp", [C, C], F32, kind="ExternalInput"))
    W1_h = ap(nc.dram_tensor("W1", [4096, C], F32, kind="ExternalInput"))
    W2T_h = ap(nc.dram_tensor("W2T", [4096, C], F32, kind="ExternalInput"))
    b1_h = ap(nc.dram_tensor("b1rs", [128, 32], F32, kind="ExternalInput"))
    b2_h = ap(nc.dram_tensor("b2row", [1, C], F32, kind="ExternalInput"))
    acol_h = ap(nc.dram_tensor("acol", [128, NG], F32, kind="ExternalInput"))
    bcol_h = ap(nc.dram_tensor("bcol", [128, NH], F32, kind="ExternalInput"))
    vbcol_h = ap(nc.dram_tensor("vbcol", [128, NG], F32, kind="ExternalInput"))
    cbc_h = ap(nc.dram_tensor("cbc", [128, C], F32, kind="ExternalInput"))

    out_h = ap(nc.dram_tensor("out", [T, C], F32, kind="ExternalOutput"))
    ktm_h = ap(nc.dram_tensor("ktm_s", [NG, T, 128], BF16,
                              kind="ExternalOutput" if stage in ("k", "kv") else "Internal"))
    v65k = "ExternalOutput" if stage == "kv" else "Internal"
    x1f_kind = "ExternalOutput" if stage == "attn" else "Internal"
    x1T_h = ap(nc.dram_tensor("x1T_s", [C, T], BF16,
                              kind="ExternalOutput" if stage == "attn" else "Internal"))
    x1f_h = ap(nc.dram_tensor("x1f_s", [T, C], F32, kind=x1f_kind))
    if stage == "kv":
        v65d_h = ap(nc.dram_tensor("v65d", [128, NT * NH * 65], BF16, kind="ExternalOutput"))

    tri_strict_np = (np.arange(128)[:, None] < np.arange(128)[None, :]).astype(ml_dtypes.bfloat16)
    tri0_np = tri_strict_np.copy()
    tri0_np[0, 0] = 1
    tri_s_h = nc.inline_tensor(tri_strict_np, "tri_s").ap()
    tri0_h = nc.inline_tensor(tri0_np, "tri0").ap()
    ones64_h = nc.inline_tensor(np.ones((1, 64), ml_dtypes.bfloat16), "ones64c").ap()
    ones128_h = nc.inline_tensor(np.ones((1, 128), ml_dtypes.bfloat16), "ones128c").ap()
    ident_h = nc.inline_tensor(np.eye(128, dtype=ml_dtypes.bfloat16), "identc").ap()

    es = contextlib.ExitStack()
    E = es.enter_context
    SB = lambda n, s, d: E(nc.sbuf_tensor(n, s, d))
    cx = Ctx(nc, es)
    st, wt = cx.step, cx.wait

    psA = E(nc.psum_tensor("psA", [128, 1024], F32))
    psB = E(nc.psum_tensor("psB", [128, 1024], F32))
    psY = [E(nc.psum_tensor(f"psY{i}", [128, 512], F32)) for i in range(2)]
    psR = E(nc.psum_tensor("psR", [128, 512], F32))
    psP = E(nc.psum_tensor("psP", [128, 512], F32))

    acol = SB("acol_sb", [128, NG], F32)
    amc = SB("amc_sb", [128, NG], F32)
    kb2 = SB("kb2_sb", [128, NH], F32)
    nkb2 = SB("nkb2_sb", [128, NH], F32)
    vbc = SB("vbc_sb", [128, NG], F32)
    tri_s = SB("tri_s_sb", [128, 128], BF16)
    tri0 = SB("tri0_sb", [128, 128], BF16)
    ones64 = SB("ones64_sb", [1, 64], BF16)
    ones128 = SB("ones128_sb", [1, 128], BF16)
    ident = SB("ident_sb", [128, 128], BF16)

    for dst, src in [(acol, acol_h), (vbc, vbcol_h), (tri_s, tri_s_h),
                     (tri0, tri0_h), (ones64, ones64_h), (ones128, ones128_h),
                     (ident, ident_h), (kb2, bcol_h)]:
        st("sync", "qs", 16)(lambda e, d=dst, s=src: e.dma_start(out=d[:], in_=s))
    ev_small = cx.ev("qs")
    wt("vector", ev_small)
    st("vector", "dve")(lambda e: e.tensor_scalar(amc[:], acol[:], -1.0, 1.0, ALU.mult, ALU.add))
    st("vector", "dve")(lambda e: e.tensor_scalar(nkb2[:], kb2[:], 10.0, 5.0, ALU.mult, ALU.min))
    ev_coefv = cx.ev("dve")
    wt("scalar", ev_coefv)
    st("scalar", "act")(lambda e: e.activation(kb2[:], nkb2[:], AF.Exp, bias=0.0, scale=2.0))
    st("scalar", "act")(lambda e: e.activation(vbc[:], vbc[:], AF.Exp, bias=0.0, scale=10.0))
    ev_coefs = cx.ev("act")
    wt("vector", ev_coefs)
    st("vector", "dve")(lambda e: e.tensor_scalar_mul(nkb2[:], kb2[:], -1.0))
    ev_nkb2 = cx.ev("dve")

    es_v65 = es.enter_context(contextlib.ExitStack())
    v65 = es_v65.enter_context(nc.sbuf_tensor("v65_sb", [128, NT * NH * 65 + 64], BF16))

    es_xt = es.enter_context(contextlib.ExitStack())
    SBx = lambda n, s, d: es_xt.enter_context(nc.sbuf_tensor(n, s, d))
    xT_bf = SBx("xT_bf", [128, KC * XTP], BF16)

    ev_xtk = {}
    KCPOOL = ("p2a", "p2b", "p2c", "p2d", "p2e", "p2f", "px0", "px1")
    for kc in range(KC):
        st("gpsimd", KCPOOL[kc], 16)(lambda e, kc=kc: e.dma_start(
            out=xT_bf[:, kc * XTP:kc * XTP + T],
            in_=xT_h[128 * kc:128 * (kc + 1), :]))
    for kc in range(KC):
        ev_xtk[kc] = st("gpsimd", KCPOOL[kc], 16)(lambda e, kc=kc: e.dma_start(
            out=Wla_bf[:, kc * C:(kc + 1) * C],
            in_=Wla_h[128 * kc:128 * (kc + 1), :]))
    ev_xt = tuple(ev_xtk[kc] for kc in range(KC))
    wt("vector", *ev_xt)
    st("vector", "dve")(lambda e: e.memset(
        xT_bf[:].rearrange("p (k c) -> p k c", c=XTP)[:, :, T:XTP], 0.0))
    ev_xtpad = cx.ev("dve")

    es_k = es.enter_context(contextlib.ExitStack())
    SBk = lambda n, s, d: es_k.enter_context(nc.sbuf_tensor(n, s, d))
    Wla_bf = SBk("Wla_bf", [128, KC * C], BF16)
    xls = SBk("xls", [128, T], F32)
    krbf = [SBk(f"krbf{i}", [128, T], BF16) for i in range(2)]
    ktm = [SBk(f"ktm{i}", [128, T], BF16) for i in range(2)]
    ksq = SBk("ksq", [128, T], BF16)
    kss = SBk("kss", [128, 64], F32)

    wt("tensor", ev_xtpad)
    ev_scan = {}
    ev_spill = {}
    ev_tposed = {}
    ev_evac = {}
    ps_free = {}
    tp_banks = [psY[0], psY[1], psR, psP]
    ev_tevac = {}
    ev_tpe = {}

    def emit_trip1_and_norm(g):
        bi = g % 2
        wt("tensor", ev_scan[g])
        for bank in range(4):
            if g >= 1:
                wt("tensor", ev_tevac[(g - 1, bank)])
            tpe = None
            for j8 in range(8):
                j = bank * 8 + j8
                tpe = st("tensor", "pe")(lambda e, j=j, j8=j8, bi=bi, bank=bank: e.transpose(
                    tp_banks[bank][:].bitcast(BF16)[:, j8 * 128:(j8 + 1) * 128],
                    krbf[bi][:, 128 * j:128 * (j + 1)], ident[:]))
            wt("vector", tpe)
            if g >= 2:
                wt("vector", ev_spill[g - 2])
            ev_tevac[(g, bank)] = st("vector", "dve")(lambda e, bank=bank, bi=bi: e.tensor_copy(
                ktm[bi][:, bank * 1024:(bank + 1) * 1024], tp_banks[bank][:].bitcast(BF16)))
        ev_tpe[g] = tpe
        wt("scalar", ev_tevac[(g, 3)])
        ev_sq = st("scalar", "act")(lambda e, bi=bi: e.activation(ksq[:], ktm[bi][:], AF.Square))
        wt("vector", ev_sq)
        ev_red = st("vector", "dve")(lambda e: e.reduce_sum(
            kss[:], ksq[:].rearrange("p (j h c) -> p j h c", h=2, c=64), axis=AX.X))
        wt("scalar", ev_red)
        st("scalar", "act")(lambda e: e.activation(kss[:], kss[:], AF.Ln))
        ev_ex = st("scalar", "act")(lambda e: e.activation(kss[:], kss[:], AF.Exp, scale=-0.5))
        wt("vector", ev_ex)
        for col in range(64):
            st("vector", "dve" if col == 63 else None)(lambda e, col=col, bi=bi: e.tensor_scalar(
                ktm[bi][:, col * 64:(col + 1) * 64], ktm[bi][:, col * 64:(col + 1) * 64],
                kss[:, col:col + 1], None, ALU.mult))
        ev_appl = cx.ev("dve")
        wt("sync", ev_appl)
        ev_spill[g] = st("sync", "pk%d" % bi, 16)(lambda e, g=g, bi=bi: e.dma_start(
            out=ktm_h[g].rearrange("(j q) c -> q j c", q=128),
            in_=ktm[bi][:].rearrange("p (j c) -> p j c", c=128)))

    for g in range(NG):
        bi = g % 2
        for tc in range(8):
            slot = (g * 8 + tc) % 4
            pst = psA if slot < 2 else psB
            half = (slot % 2) * 512
            if slot in ps_free:
                wt("tensor", ps_free[slot])
            ev = None
            for kc in range(KC):
                if g == 0 and tc == 0:
                    wt("tensor", ev_xtk[kc])
                ev = st("tensor", "pe" if kc == KC - 1 else None)(
                    lambda e, kc=kc, g=g, tc=tc, pst=pst, half=half: e.matmul(
                        pst[:, half:half + 512],
                        Wla_bf[:, kc * C + g * 128: kc * C + (g + 1) * 128],
                        xT_bf[:, kc * XTP + tc * 512: kc * XTP + (tc + 1) * 512],
                        start=(kc == 0), stop=(kc == KC - 1)))
            wt("vector", ev)
            ps_free[slot] = st("vector", "dve")(
                lambda e, g=g, tc=tc, pst=pst, half=half: e.tensor_scalar(
                    xls[:, tc * 512:(tc + 1) * 512], pst[:, half:half + 512],
                    amc[:, g:g + 1], None, ALU.mult))
        ev_evac[g] = cx.ev("dve")
        if g >= 2:
            wt("vector", ev_tpe[g - 2])
        ev_scan[g] = st("vector", "dve")(lambda e, bi=bi, g=g: e.tensor_tensor_scan(
            krbf[bi][:], acol[:, g:g + 1].broadcast_to([128, T]), xls[:],
            0.0, ALU.mult, ALU.add))
        if g >= 1:
            emit_trip1_and_norm(g - 1)
    emit_trip1_and_norm(NG - 1)

    es_k.close()

    es_v = es.enter_context(contextlib.ExitStack())
    SBv = lambda n, s, d: es_v.enter_context(nc.sbuf_tensor(n, s, d))
    ev_k_done = (cx.ev("dve"), ev_spill[NG - 1])
    Wvc_bf = SBv("Wvc_bf", [128, KC * C], BF16)
    Wv1_bf = SBv("Wv1_bf", [128, KC * C], BF16)
    wtmp = SBv("wtmp", [128, C], F32)
    vsq = [SBv(f"vsq{i}", [128, C], F32) for i in range(2)]
    vss = [SBv(f"vss{i}", [128, NH], F32) for i in range(2)]
    vsb = [SBv(f"vsb{i}", [128, C], F32) for i in range(2)]
    cb = SBv("cb_sb", [128, C], F32)
    cmb = SBv("cmb_sb", [128, C], F32)
    wt("sync", ev_k_done[0])
    st("sync", "qs", 16)(lambda e: e.dma_start(out=cb[:], in_=cbc_h))
    ev_cbl = cx.ev("qs")
    wt("vector", ev_cbl)
    st("vector", "dve")(lambda e: e.tensor_scalar(cmb[:], cb[:], -1.0, 1.0, ALU.mult, ALU.add))

    ev_wvd = {}
    for kc in range(KC):
        wt("sync", ev_k_done[0] if kc == 0 else ev_wvd[kc - 1])
        st("sync", "qs", 16)(lambda e, kc=kc: e.dma_start(out=wtmp[:], in_=Wv_h[128 * kc:128 * (kc + 1), :]))
        ev_l = cx.ev("qs")
        wt("vector", ev_l)
        st("vector", "dve")(lambda e, kc=kc: e.tensor_mul(Wvc_bf[:, kc * C:(kc + 1) * C], wtmp[:], cb[:]))
        ev_wvd[kc] = st("vector", "dve")(lambda e, kc=kc: e.tensor_mul(Wv1_bf[:, kc * C:(kc + 1) * C], wtmp[:], cmb[:]))
    st("vector", "dve")(lambda e: e.memset(v65[:], 1.0))
    ev_v65init = cx.ev("dve")

    ev_vdone = {}
    wt("tensor", ev_wvd[KC - 1])
    for i in range(NT):
        bi = i % 2
        pst = psA if i % 2 == 0 else psB
        if i >= 2:
            wt("tensor", ev_vdone[i - 2])
        ev = None
        first = True
        for half in range(2):
            for kc in range(KC):
                for (w, sh) in ((Wvc_bf, 0), (Wv1_bf, 1)):
                    last = (half == 1 and kc == KC - 1 and sh == 1)
                    ev = st("tensor", "pe" if last else None)(
                        lambda e, kc=kc, w=w, sh=sh, half=half, i=i, pst=pst, first=first,
                               lst=(kc == KC - 1 and sh == 1): e.matmul(
                            pst[:, half * 512:half * 512 + 512],
                            xT_bf[:, kc * XTP + i * 128 + sh: kc * XTP + i * 128 + sh + 128],
                            w[:, kc * C + half * 512: kc * C + half * 512 + 512],
                            start=first, stop=lst))
                    first = False
            first = True
        wt("scalar", ev)
        ev_sq = st("scalar", "act")(lambda e, bi=bi, pst=pst: e.activation(vsq[bi][:], pst[:], AF.Square))
        wt("vector", ev_sq)
        ev_red = st("vector", "dve")(lambda e, bi=bi: e.reduce_sum(
            vss[bi][:], vsq[bi][:].rearrange("p (h c) -> p h c", c=64), axis=AX.X))
        wt("scalar", ev_red)
        st("scalar", "act")(lambda e, bi=bi: e.activation(vss[bi][:], vss[bi][:], AF.Ln))
        ev_ex = st("scalar", "act")(lambda e, bi=bi: e.activation(vss[bi][:], vss[bi][:], AF.Exp, scale=-0.5))
        wt("vector", ev_ex)
        st("vector", "dve")(lambda e, bi=bi: e.tensor_copy(
            vsb[bi][:].rearrange("p (h c) -> p h c", c=64),
            vss[bi][:].rearrange("p (h c) -> p h c", c=1).broadcast_to([128, NH, 64])))
        if i == 0:
            wt("vector", ev_v65init)
        ev_vdone[i] = st("vector", "dve")(lambda e, bi=bi, i=i, pst=pst: e.tensor_mul(
            v65[:, i * NH * 65:(i + 1) * NH * 65].rearrange("p (h c) -> p h c", c=65)[:, :, 0:64],
            pst[:].rearrange("p (h c) -> p h c", c=64),
            vsb[bi][:].rearrange("p (h c) -> p h c", c=64)))

    if stage == "kv":
        wt("sync", ev_vdone[NT - 1])
        st("sync", "qs", 16)(lambda e: e.dma_start(out=v65d_h, in_=v65[:]))
        ev_fin = cx.ev("qs")
        wt("sync", ev_fin)
        es_v.close(); es_xt.close()
        finish(cx)
        return nc
    es_v.close(); es_xt.close()
    if stage == "k":
        wt("sync", ev_spill[NG - 1])
        finish(cx)
        return nc

    es_c = es.enter_context(contextlib.ExitStack())
    SBc = lambda n, s, d: es_c.enter_context(nc.sbuf_tensor(n, s, d))
    kT = SBc("kT_sb", [128, NG * T], BF16)
    Wp_bf = SBc("Wp_bf", [128, KC * C], BF16)
    es_w = contextlib.ExitStack()
    wtmp2 = es_w.enter_context(nc.sbuf_tensor("wtmp2", [128, C], F32))
    P_sb = None

    ev_1a_done = (cx.ev("dve"), cx.ev("pe"), cx.ev("act"))

    ev_kld = {}
    wt("sync", ev_1a_done[0], ev_1a_done[1])

    def reload_wave(gq):
        for kj in range(4 * gq, 4 * gq + 4):
            for g in range(NG):
                st("sync", "qt", 16)(lambda e, g=g, kj=kj: e.dma_start_transpose(
                    kT[:, g * T + 128 * kj: g * T + 128 * (kj + 1)],
                    ktm_h[g, 128 * kj:128 * (kj + 1), :]))
        ev_kld[gq] = (cx.ev("qt"),)

    wt("sync", ev_spill[NG - 1], ev_spill[NG - 2])
    reload_wave(0)
    reload_wave(1)

    ev_wpd = {}
    for gc in range(KC):
        if gc > 0:
            wt("sync", ev_wpd[gc - 1])
        st("sync", "qs", 16)(lambda e, gc=gc: e.dma_start(out=wtmp2[:], in_=Wp_h[128 * gc:128 * (gc + 1), :]))
        ev_l = cx.ev("qs")
        wt("vector", ev_l)
        ev_wpd[gc] = st("vector", "dve")(lambda e, gc=gc: e.tensor_scalar(
            Wp_bf[:, gc * C:(gc + 1) * C], wtmp2[:], vbc[:, gc:gc + 1], None, ALU.mult))
    es_w.close()
    P_sb = [SBc(f"P{i}", [128, 1024], BF16) for i in range(2)]
    ynT = [SBc(f"ynT{i}", [128, NG * 512], BF16) for i in range(2)]
    rrow = SBc("rrow", [1, 512], F32)
    rhi = [SBc(f"rhi{i}", [1, 512], BF16) for i in range(2)]
    rlo = [SBc(f"rlo{i}", [1, 512], BF16) for i in range(2)]
    rbc = SBc("rbc", [64, 512], F32)
    yst = [SBc(f"yst{i}", [64, 512], BF16) for i in range(2)]
    lrow = SBc("lrow2", [1, 1024], F32)
    x_sb = [SBc(f"x_sb{i}", [128, C], F32) for i in range(2)]
    h_sb = [SBc(f"h_sb{i}", [128, C], F32) for i in range(2)]
    hscr = SBc("hscr", [128, C], BF16)
    stat = SBc("stat", [128, 8], F32)
    x1f_sb = SBc("x1f", [128, C], F32)
    x1b_sb = SBc("x1b", [128, C], BF16)
    x1stage = SBc("x1stage", [128, C], BF16)

    x_ev = {}

    def load_x(ti):
        wt("sync", ev_1a_done[0])
        return st("sync", "px%d" % (ti % 2), 16)(lambda e, ti=ti: e.dma_start(
            out=x_sb[ti % 2][:], in_=x_h[128 * ti:128 * (ti + 1), :]))

    x_ev[0] = load_x(0)
    x_ev[1] = load_x(1)

    gS = 0
    pend_lpath = []
    pend_lsc = []
    pend_pv = []
    pend_proj = []
    rr_par = lambda hi_: hi_ % 2
    rr_t = lambda hb: (rrow, rhi[hb], rlo[hb])

    def emit_pv(key, kj, pbuf, h_, hi_, ypar_, nkj_, gq_, slot, off, mev):
        if kj == 0 and hi_ >= 2:
            wt("tensor", ev_yev[hi_ - 2])
        wt("tensor", ev_expg[key])
        if mev is not None:
            wt("tensor", mev)
        pvev = st("tensor", "pe")(
            lambda e, kj=kj, h=h_, off=off, pbuf=pbuf, ypar=ypar_, slot=slot,
                   fi=(kj == 0), la=(kj == nkj_ - 1): e.matmul(
                psY[ypar][:, off:512],
                v65[:, kj * NH * 65 + h * 65: kj * NH * 65 + h * 65 + 128],
                pbuf[:, slot * 512 + off: (slot + 1) * 512],
                start=fi, stop=la))
        ev_pvg[key] = pvev
        return pvev

    def emit_lsc(hi_, g_, e64_, par_, ypar_, nkj_):
        hb = rr_par(hi_)
        wt("scalar", ev_yev[hi_])
        if hi_ >= 1:
            wt("scalar", ev_rlo_d[hi_ - 1])
        st("scalar", "act")(lambda e, hb=hb: e.activation(
            rr_t(hb)[0][:], lrow[0:1, hb * 512:hb * 512 + 512], AF.Ln))
        ev_r = st("scalar", "act")(lambda e, hb=hb: e.activation(
            rr_t(hb)[0][:], rr_t(hb)[0][:], AF.Exp, scale=-1.0))
        ev_expr[hi_] = ev_r
        wt("vector", ev_r)
        wt("vector", ev_lbcd.get(hi_ - 2))
        st("vector", "dve")(lambda e, hb=hb: e.tensor_copy(rr_t(hb)[1][:], rr_t(hb)[0][:]))
        ev_rlo_d[hi_] = st("vector", "dve")(lambda e, hb=hb: e.tensor_sub(
            rr_t(hb)[2][:], rr_t(hb)[0][:], rr_t(hb)[1][:]))
        pend_lpath.append((hi_, g_, e64_, par_, ypar_))

    def emit_lpath_te(hi_, g_, e64_, par_, ypar_):
        hb = rr_par(hi_)
        wt("tensor", ev_rlo_d[hi_])
        if hi_ >= 1:
            wt("tensor", ev_ynorm[hi_ - 1])
        if (hi_ % NH) == 0 and hi_ >= NH:
            wt("tensor", ev_lnv.get(4 * (hi_ // NH) - 1))
        st("tensor", None)(lambda e, hb=hb: e.matmul(psR[0:64, :], ones64[:], rr_t(hb)[1][:], start=True, stop=False))
        ev_lbc = st("tensor", "pe")(lambda e, hb=hb: e.matmul(psR[0:64, :], ones64[:], rr_t(hb)[2][:], start=False, stop=True))
        ev_lbcd[hi_] = ev_lbc
        wt("vector", ev_lbc)
        st("vector", "dve")(lambda e: e.tensor_copy(rbc[:], psR[0:64, :]))
        if (hi_ % NH) == 0 and hi_ // NH >= 2:
            wt("vector", ev_projmm[4 * (hi_ // NH - 2) + 3])
        ev_ynorm[hi_] = st("vector", "dve")(lambda e, g=g_, e64=e64_, par=par_, hb=rr_par(hi_): e.tensor_mul(
            ynT[par][e64:e64 + 64, g * 512:(g + 1) * 512], yst[hb][:], rbc[:]))

    def emit_projgq(gq, par, ev_yn_all):
        for qt in range(4):
            ti = 4 * gq + qt
            bi = ti % 2
            wt("tensor", ev_yn_all)
            if ti >= 1:
                wt("tensor", ev_lnv.get(ti - 1))
            pe_ev = None
            for half, pp in ((0, psP), (1, psR)):
                for gc in range(KC):
                    pe_ev = st("tensor", "pe" if gc == KC - 1 else None)(
                        lambda e, gc=gc, half=half, qt=qt, par=par, pp=pp: e.matmul(
                            pp[:, 0:512],
                            ynT[par][:, gc * 512 + qt * 128: gc * 512 + (qt + 1) * 128],
                            Wp_bf[:, gc * C + half * 512: gc * C + half * 512 + 512],
                            start=(gc == 0), stop=(gc == KC - 1)))
            ev_projmm[ti] = pe_ev
            wt("vector", pe_ev, x_ev[ti])
            st("vector", "dve")(lambda e, bi=bi: e.scalar_tensor_tensor(
                h_sb[bi][:, 0:512], psP[:, 0:512], 1.0, x_sb[bi][:, 0:512],
                ALU.mult, ALU.add, accum_out=stat[:, 0:1]))
            st("vector", "dve")(lambda e, bi=bi: e.scalar_tensor_tensor(
                h_sb[bi][:, 512:1024], psR[:, 0:512], 1.0, x_sb[bi][:, 512:1024],
                ALU.mult, ALU.add, accum_out=stat[:, 1:2]))
            st("vector", "dve")(lambda e, bi=bi: e.scalar_tensor_tensor(
                hscr[:], h_sb[bi][:], 1.0, h_sb[bi][:], ALU.mult, ALU.mult,
                accum_out=stat[:, 2:3]))
            e1 = st("vector", "dve")(lambda e: e.tensor_add(stat[:, 0:1], stat[:, 0:1], stat[:, 1:2]))
            wt("vector", e1)
            e2 = st("vector", "dve")(lambda e: e.tensor_scalar_mul(stat[:, 3:4], stat[:, 0:1], -1.0 / C))
            wt("vector", e2)
            e3 = st("vector", "dve")(lambda e: e.tensor_mul(stat[:, 4:5], stat[:, 3:4], stat[:, 3:4]))
            wt("vector", e3)
            e4 = st("vector", "dve")(lambda e: e.scalar_tensor_tensor(
                stat[:, 5:6], stat[:, 2:3], 1.0 / C, stat[:, 4:5], ALU.mult, ALU.subtract))
            wt("vector", e4)
            ev_var = st("vector", "dve")(lambda e: e.tensor_scalar_add(stat[:, 5:6], stat[:, 5:6], EPS))
            wt("scalar", ev_var)
            st("scalar", "act")(lambda e: e.activation(stat[:, 6:7], stat[:, 5:6], AF.Ln))
            ev_rstd = st("scalar", "act")(lambda e: e.activation(stat[:, 6:7], stat[:, 6:7], AF.Exp, scale=-0.5))
            wt("vector", ev_rstd)
            if ti >= 1:
                wt("vector", ev_spl_f[ti - 1])
            ev_x1 = st("vector", "dve")(lambda e, bi=bi: e.tensor_scalar(
                x1f_sb[:], h_sb[bi][:], stat[:, 3:4], stat[:, 6:7], ALU.add, ALU.mult))
            ev_lnv[ti] = ev_x1
            if ti >= 1:
                wt("vector", ev_t_spl[ti - 1])
            ev_x1b = st("vector", "dve")(lambda e: e.tensor_copy(x1b_sb[:], x1f_sb[:]))
            wt("sync", ev_x1b)
            ev_spl_f[ti] = st("sync", "pf%d" % (ti % 2), 16)(lambda e, ti=ti: e.dma_start(
                out=x1f_h[128 * ti:128 * (ti + 1), :], in_=x1f_sb[:]))
            if ti >= 1:
                wt("sync", ev_st_spl[ti - 1])
            for gc in range(NG):
                st("sync", "qt", 16)(lambda e, gc=gc: e.dma_start_transpose(
                    x1stage[:, gc * 128:(gc + 1) * 128], x1b_sb[:, gc * 128:(gc + 1) * 128]))
            ev_t_spl[ti] = cx.ev("qt")
            wt("sync", ev_t_spl[ti])
            ev_st_spl[ti] = st("sync", "pst", 16)(lambda e, ti=ti: e.dma_start(
                out=x1T_h.rearrange("(gc p) t -> p gc t", p=128)[:, :, 128 * ti:128 * (ti + 1)],
                in_=x1stage[:].rearrange("p (gc c) -> p gc c", c=128)))
            if ti + 2 < NT:
                wt("sync", ev_x1)
                x_ev[ti + 2] = st("sync", "px%d" % (ti % 2), 16)(lambda e, ti=ti: e.dma_start(
                    out=x_sb[ti % 2][:], in_=x_h[128 * (ti + 2):128 * (ti + 3), :]))

    ev_expg = {}
    ev_smm = {}
    ev_pvg = {}
    ev_ynorm = {}
    ev_projmm = {}
    ev_lnv = {}
    ev_rlo_d = {}
    ev_yev = {}
    ev_expr = {}
    ev_lbcd = {}
    ev_spl_f = {}
    ev_t_spl = {}
    ev_st_spl = {}
    ev_ost = {}
    hi = 0
    for gq in range(QG):
        nkj = 4 * gq + 4
        par = gq % 2
        if gq + 2 < QG:
            reload_wave(gq + 2)
        for hp in range(NH // 2):
            g = hp
            nb = (nkj + 1) // 2
            if hp == 1 and pend_proj:
                emit_projgq(*pend_proj.pop(0))
            if hi == 0:
                wt("scalar", ev_nkb2)
            for kj in range(nkj):
                slot = kj % 2
                off = 128 * max(0, kj - 4 * gq)
                for side, e64, pst, pbuf in ((0, 0, psA, P_sb[0]), (1, 64, psB, P_sb[1])):
                    hix = hi + side
                    if kj >= 2:
                        wt("tensor", ev_expg[(hix, kj - 2)])
                    if hi == 0 and kj == 0 and side == 0:
                        wt("tensor", ev_kld[gq][0], ev_wpd[KC - 1])
                    elif kj == 0 and hp == 0 and side == 0:
                        wt("tensor", ev_kld[gq][0])
                    ev_smm[(hix, kj)] = st("tensor", "pe")(
                        lambda e, g=g, e64=e64, kj=kj, gq=gq, off=off, pst=pst, slot=slot: e.matmul(
                            pst[:, slot * 512 + off: (slot + 1) * 512],
                            kT[e64:e64 + 64, g * T + 128 * kj: g * T + 128 * (kj + 1)],
                            kT[e64:e64 + 64, g * T + 512 * gq + off: g * T + 512 * (gq + 1)],
                            start=True, stop=True, tile_position=(e64, 0)))
                if len(pend_pv) >= 2:
                    emit_pv(*pend_pv.pop(0))
                    emit_pv(*pend_pv.pop(0))
                for side, e64, pst, pbuf in ((0, 0, psA, P_sb[0]), (1, 64, psB, P_sb[1])):
                    h = 2 * hp + side
                    hix = hi + side
                    wt("scalar", ev_smm[(hix, kj)])
                    if kj >= 2:
                        wt("scalar", ev_pvg[(hix, kj - 2)])
                    ev_expg[(hix, kj)] = st("scalar", "act")(
                        lambda e, pst=pst, pbuf=pbuf, h=h, slot=slot, off=off: e.activation(
                            pbuf[:, slot * 512 + off: (slot + 1) * 512],
                            pst[:, slot * 512 + off: (slot + 1) * 512], AF.Exp,
                            bias=nkb2[:, h:h + 1], scale=kb2[:, h:h + 1]))
                    mev = None
                    if kj >= 4 * gq:
                        tri = tri0 if (gq == 0 and kj == 0) else tri_s
                        wt("vector", ev_expg[(hix, kj)])
                        mev = st("vector", "dve")(lambda e, pbuf=pbuf, slot=slot, off=off, tri=tri: e.tensor_mul(
                            pbuf[:, slot * 512 + off: slot * 512 + off + 128],
                            pbuf[:, slot * 512 + off: slot * 512 + off + 128], tri[:]))
                    pend_pv.append(((hix, kj), kj, pbuf, h, hix, side, nkj, gq, slot, off, mev))
                    if side == 1 and kj in (1, 2) and pend_lsc:
                        emit_lsc(*pend_lsc.pop(0))
                    if side == 1 and kj in (3, 4) and pend_lpath:
                        emit_lpath_te(*pend_lpath.pop(0))
            while pend_pv:
                pvev = emit_pv(*pend_pv.pop(0))
            while pend_lpath:
                emit_lpath_te(*pend_lpath.pop(0))
            for side in range(2):
                hix = hi + side
                hb = rr_par(hix)
                wt("vector", ev_pvg[(hix, nkj - 1)])
                if hix >= 2:
                    wt("vector", ev_expr.get(hix - 2))
                st("vector", "dve")(lambda e, ypar=side, hb=hb: e.tensor_copy(
                    yst[hb][:], psY[ypar][0:64, :]))
                ev_yev[hix] = st("vector", "dve")(lambda e, ypar=side, hb=hb: e.tensor_copy(
                    lrow[0:1, hb * 512:hb * 512 + 512], psY[ypar][64:65, :]))
                pend_lsc.append((hix, g, side * 64, par, side, nkj))
            hi += 2
        while pend_lsc:
            emit_lsc(*pend_lsc.pop(0))
        while pend_lpath:
            emit_lpath_te(*pend_lpath.pop(0))
        if gq == 0:
            wt("vector", ev_ynorm[hi - 1])
            for gc in range(NG):
                st("vector", "dve")(lambda e, gc=gc: e.memset(ynT[0][:, gc * 512:gc * 512 + 1], 0.0))
        ev_yn_all = cx.ev("dve")
        pend_proj.append((gq, par, ev_yn_all))

    while pend_proj:
        emit_projgq(*pend_proj.pop(0))

    ev_1c_spill_done = cx.ev("qs")
    ev_1c_pools = (cx.ev("pf0"), cx.ev("pf1"), cx.ev("pst"), cx.ev("px0"), cx.ev("px1"))
    if stage == "attn":
        wt("sync", ev_1c_spill_done)
        es_c.close()
        finish(cx)
        return nc
    es_c.close()
    es_v65.close()

    es_2 = es.enter_context(contextlib.ExitStack())
    SB2 = lambda n, s, d: es_2.enter_context(nc.sbuf_tensor(n, s, d))
    W2T_bf = SB2("W2T_bf", [128, 32 * C], BF16)
    w1s = [SB2(f"w1s{i}", [128, 32 * 128], BF16) for i in range(2)]
    FF_bf = SB2("FF_bf", [128, KC * C], BF16)
    bias2 = SB2("bias2", [1, C], BF16)
    b1_bf = SB2("b1_bf", [128, 32], BF16)
    x1t_sb = [SB2(f"x1t{i}", [128, NG * 128], BF16) for i in range(3)]
    b2row = SB2("b2row_sb", [1, C], F32)
    x2f_sb = [SB2(f"x2f{i}", [128, C], F32) for i in range(3)]
    h2_sb = [SB2(f"h2_sb{i}", [128, C], F32) for i in range(2)]
    h2scr = SB2("h2scr", [128, C], BF16)
    st2 = SB2("st2", [128, 8], F32)
    o_sb = [SB2(f"o_sb{i}", [128, C], F32) for i in range(2)]

    ev_1c_done = (cx.ev("pe"), cx.ev("dve"), cx.ev("act"), ev_1c_spill_done)
    wt("sync", *ev_1c_done[:3])
    st("sync", "qs", 16)(lambda e: e.dma_start(out=b2row[:], in_=b2_h))
    ev_b2l = cx.ev("qs")
    wt("gpsimd", *ev_1c_done[:3])
    wstg = [SB2(f"wstg{i}", [128, 4 * C], F32) for i in range(2)]
    ev_w2c = {}
    for q4 in range(8):
        if q4 >= 2:
            wt("sync", ev_w2c[q4 - 2])
        st("sync", ("pf0", "pf1")[q4 % 2], 16)(lambda e, q4=q4: e.dma_start(
            out=wstg[q4 % 2][:].rearrange("p (oc c) -> p oc c", c=C),
            in_=W2T_h[q4 * 512:(q4 + 1) * 512, :].rearrange("(oc p) c -> p oc c", p=128)))
        evl = cx.ev(("pf0", "pf1")[q4 % 2])
        wt("vector", evl)
        ev_w2c[q4] = st("vector", "dve")(lambda e, q4=q4: e.tensor_copy(
            W2T_bf[:, q4 * 4 * C:(q4 + 1) * 4 * C], wstg[q4 % 2][:]))
    ev_w2 = ev_w2c[7]
    st("gpsimd", "qg", 16)(lambda e: e.dma_start(out=b1_bf[:], in_=b1_h))
    ev_b1 = cx.ev("qg")
    wt("tensor", ev_w2, ev_b1)
    for half, pb in ((0, psY[0]), (1, psY[1])):
        for oc in range(32):
            st("tensor", "pe" if oc == 31 else None)(lambda e, oc=oc, half=half, pb=pb: e.matmul(
                pb[0:1, :], b1_bf[:, oc:oc + 1],
                W2T_bf[:, oc * C + half * 512: oc * C + half * 512 + 512],
                start=(oc == 0), stop=(oc == 31)))
    ev_b2mm = cx.ev("pe")
    wt("vector", ev_b2mm, ev_b2l)
    st("vector", "dve")(lambda e: e.tensor_add(bias2[0:1, 0:512], psY[0][0:1, :], b2row[0:1, 0:512]))
    ev_bias2 = st("vector", "dve")(lambda e: e.tensor_add(bias2[0:1, 512:1024], psY[1][0:1, :], b2row[0:1, 512:1024]))

    ev_ffd = {}
    ev_w1l = {}
    ev_ffmm = {}

    w1f = SB2("w1f", [128, 32 * 128], F32)

    def load_w1(hc):
        if hc >= 1:
            wt("sync", ev_w1l[hc - 1])
        for h2_ in range(2):
            st("sync", "qs", 16)(lambda e, hc=hc, h2_=h2_: e.dma_start(
                out=w1f[:, h2_ * 16 * 128:(h2_ + 1) * 16 * 128].rearrange("p (oc c) -> p oc c", c=128),
                in_=W1_h[h2_ * 2048:(h2_ + 1) * 2048, 128 * hc:128 * (hc + 1)].rearrange("(oc p) c -> p oc c", p=128)))
        evl = cx.ev("qs")
        wt("vector", evl)
        if hc >= 2:
            wt("vector", ev_ffmm[hc - 2])
        ev_w1l[hc] = st("vector", "dve")(lambda e, hc=hc: e.tensor_copy(w1s[hc % 2][:], w1f[:]))

    load_w1(0)
    load_w1(1)
    for hc in range(KC):
        wt("tensor", ev_w1l[hc])
        if hc >= 2:
            wt("tensor", ev_ffd[hc - 2])
        pfold = psA if hc % 2 == 0 else psB
        ev = None
        for half in range(2):
            for oc in range(32):
                ev = st("tensor", "pe" if (oc == 31 and half == 1) else None)(
                    lambda e, oc=oc, hc=hc, half=half, pfold=pfold: e.matmul(
                        pfold[:, half * 512:half * 512 + 512],
                        w1s[hc % 2][:, oc * 128:(oc + 1) * 128],
                        W2T_bf[:, oc * C + half * 512: oc * C + half * 512 + 512],
                        start=(oc == 0), stop=(oc == 31)))
        ev_ffmm[hc] = ev
        wt("vector", ev)
        ev_ffd[hc] = st("vector", "dve")(lambda e, hc=hc, pfold=pfold: e.tensor_copy(FF_bf[:, hc * C:(hc + 1) * C], pfold[:]))
        if hc + 2 < KC:
            load_w1(hc + 2)

    if stage == "ffa":
        ffd_h = ap(nc.dram_tensor("ffd", [128, KC * C], BF16, kind="ExternalOutput"))
        b2d_h = ap(nc.dram_tensor("b2d", [1, C], BF16, kind="ExternalOutput"))
        wt("sync", ev_ffd[KC - 1], ev_bias2)
        st("sync", "qs", 16)(lambda e: e.dma_start(out=ffd_h, in_=FF_bf[:]))
        st("sync", "qs", 16)(lambda e: e.dma_start(out=b2d_h, in_=bias2[:]))
        evf = cx.ev("qs")
        wt("sync", evf)
        es_2.close()
        finish(cx)
        return nc

    ev_x1t = {}
    ev_x1f = {}
    ev_oev = {}
    ev_ost2 = {}
    wt("sync", ev_1c_done[3], *ev_1c_pools)

    def load_tile2(ti):
        e1 = st("sync", ("p2a", "p2b", "p2c")[ti % 3], 16)(lambda e, ti=ti: e.dma_start(
            out=x1t_sb[ti % 3][:].rearrange("p (gc c) -> p gc c", c=128),
            in_=x1T_h.rearrange("(gc p) t -> p gc t", p=128)[:, :, 128 * ti:128 * (ti + 1)]))
        ev_x1t[ti] = (e1,)
        ev_x1f[ti] = st("sync", ("p2d", "p2e", "p2f")[ti % 3], 16)(lambda e, ti=ti: e.dma_start(
            out=x2f_sb[ti % 3][:], in_=x1f_h[128 * ti:128 * (ti + 1), :]))

    load_tile2(0)
    load_tile2(1)
    load_tile2(2)
    for ti in range(NT):
        bi = ti % 2
        pst2 = psB if ti % 2 == 0 else psA
        wt("tensor", ev_x1t[ti][0])
        if ti == 0:
            wt("tensor", ev_ffd[KC - 1])
        if ti >= 2:
            wt("tensor", ev_oev[ti - 2])
        if PARANOID and ti >= 1:
            wt("tensor", ev_oev[ti - 1])
            wt("tensor", ev_ost2.get(ti - 1))
        pe_ev = None
        for half in range(2):
            for gc in range(KC):
                pe_ev = st("tensor", None)(lambda e, gc=gc, half=half, ti=ti, bi=bi, pst2=pst2: e.matmul(
                    pst2[:, half * 512:half * 512 + 512],
                    x1t_sb[ti % 3][:, gc * 128:(gc + 1) * 128],
                    FF_bf[:, gc * C + half * 512: gc * C + half * 512 + 512],
                    start=(gc == 0), stop=False))
            pe_ev = st("tensor", "pe" if half == 1 else None)(
                lambda e, half=half, pst2=pst2: e.matmul(
                    pst2[:, half * 512:half * 512 + 512], ones128[:],
                    bias2[0:1, half * 512:half * 512 + 512], start=False, stop=True))
        if ti == 0:
            wt("vector", ev_bias2)
        wt("vector", pe_ev, ev_x1f[ti])
        st("vector", "dve")(lambda e, bi=bi, ti=ti, pst2=pst2: e.scalar_tensor_tensor(
            h2_sb[bi][:, 0:512], pst2[:, 0:512], 1.0, x2f_sb[ti % 3][:, 0:512],
            ALU.mult, ALU.add, accum_out=st2[:, 0:1]))
        st("vector", "dve")(lambda e, bi=bi, ti=ti, pst2=pst2: e.scalar_tensor_tensor(
            h2_sb[bi][:, 512:1024], pst2[:, 512:1024], 1.0, x2f_sb[ti % 3][:, 512:1024],
            ALU.mult, ALU.add, accum_out=st2[:, 1:2]))
        st("vector", "dve")(lambda e, bi=bi: e.scalar_tensor_tensor(
            h2scr[:], h2_sb[bi][:], 1.0, h2_sb[bi][:], ALU.mult, ALU.mult,
            accum_out=st2[:, 2:3]))
        e1 = st("vector", "dve")(lambda e: e.tensor_add(st2[:, 0:1], st2[:, 0:1], st2[:, 1:2]))
        wt("vector", e1)
        e2 = st("vector", "dve")(lambda e: e.tensor_scalar_mul(st2[:, 3:4], st2[:, 0:1], -1.0 / C))
        wt("vector", e2)
        e3 = st("vector", "dve")(lambda e: e.tensor_mul(st2[:, 4:5], st2[:, 3:4], st2[:, 3:4]))
        wt("vector", e3)
        e4 = st("vector", "dve")(lambda e: e.scalar_tensor_tensor(
            st2[:, 5:6], st2[:, 2:3], 1.0 / C, st2[:, 4:5], ALU.mult, ALU.subtract))
        wt("vector", e4)
        ev_var = st("vector", "dve")(lambda e: e.tensor_scalar_add(st2[:, 5:6], st2[:, 5:6], EPS))
        wt("scalar", ev_var)
        st("scalar", "act")(lambda e: e.activation(st2[:, 6:7], st2[:, 5:6], AF.Ln))
        ev_rstd = st("scalar", "act")(lambda e: e.activation(st2[:, 6:7], st2[:, 6:7], AF.Exp, scale=-0.5))
        wt("vector", ev_rstd)
        if ti >= 2:
            wt("vector", ev_ost2[ti - 2])
        ev_o = st("vector", "dve")(lambda e, bi=bi: e.tensor_scalar(
            o_sb[bi][:], h2_sb[bi][:], st2[:, 3:4], st2[:, 6:7], ALU.add, ALU.mult))
        ev_oev[ti] = ev_o
        wt("sync", ev_o)
        st("sync", "qs", 16)(lambda e, ti=ti, bi=bi: e.dma_start(
            out=out_h[128 * ti:128 * (ti + 1), :], in_=o_sb[bi][:]))
        ev_ost2[ti] = cx.ev("qs")
        if ti + 3 < NT:
            wt("sync", ev_oev[ti])
            load_tile2(ti + 3)
    ev_outd = cx.ev("qs")
    wt("sync", ev_outd)
    es_2.close()
    finish(cx)
    return nc


def finish(cx):
    nc = cx.nc
    with nc.Block() as block:
        @block.tensor
        def _(t):
            for f in cx.prog["tensor"]:
                f(t)

        @block.vector
        def _(v):
            for f in cx.prog["vector"]:
                f(v)

        @block.scalar
        def _(s):
            for f in cx.prog["scalar"]:
                f(s)

        @block.gpsimd
        def _(g):
            for f in cx.prog["gpsimd"]:
                f(g)

        @block.sync
        def _(s):
            for f in cx.prog["sync"]:
                f(s)


def _core_inputs(b, ins):
    xb = np.ascontiguousarray(ins["L0_attn_x"][b])
    la = np.asarray(ins["L0_attn_la_coef"], dtype=np.float32)
    kb = np.asarray(ins["L0_attn_kernel_beta"], dtype=np.float32)
    vb = np.asarray(ins["L0_attn_value_beta"], dtype=np.float32)
    vc = np.asarray(ins["L0_attn_v_coef"], dtype=np.float32)
    hog = lambda g: np.concatenate([np.full(64, 2 * g), np.full(64, 2 * g + 1)])
    acol = np.stack([la[hog(g)] for g in range(NG)], axis=1)
    vbcol = np.stack([vb[hog(g)] for g in range(NG)], axis=1)
    bcol = np.broadcast_to(kb[None, :], (128, NH)).copy()
    cbc = np.broadcast_to(np.repeat(vc, 64)[None, :], (128, C)).copy()
    b1 = np.asarray(ins["L0_ffn_b1"][b], dtype=np.float32)
    return dict(
        x=xb,
        xT=np.ascontiguousarray(xb.T),
        Wla=np.ascontiguousarray(ins["L0_attn_W_la"]),
        Wv=np.ascontiguousarray(ins["L0_attn_W_v"]),
        Wp=np.ascontiguousarray(ins["L0_attn_W_proj"]),
        W1=np.ascontiguousarray(ins["L0_ffn_W1"][b]),
        W2T=np.ascontiguousarray(np.asarray(ins["L0_ffn_W2"][b]).T),
        b1rs=np.ascontiguousarray(b1.reshape(32, 128).T),
        b2row=np.ascontiguousarray(np.asarray(ins["L0_ffn_b2"][b], dtype=np.float32)[None, :]),
        acol=np.ascontiguousarray(acol.astype(np.float32)),
        bcol=np.ascontiguousarray(bcol.astype(np.float32)),
        vbcol=np.ascontiguousarray(vbcol.astype(np.float32)),
        cbc=np.ascontiguousarray(cbc.astype(np.float32)),
    )


def kernel(**inputs):
    ins = {k: np.asarray(v) for k, v in inputs.items()}
    nc = build("full")
    in_maps = [_core_inputs(b, ins) for b in range(8)]
    res = run_bass_kernel_spmd(nc, in_maps, core_ids=list(range(8)))
    return np.stack([np.asarray(r["out"], dtype=np.float32) for r in res.results], axis=0)
